# version 16
# baseline (speedup 1.0000x reference)
import contextlib
import math
import numpy as np
import ml_dtypes
import concourse.bass as bass
import concourse.mybir as mybir
from concourse.bass_utils import run_bass_kernel_spmd

F32 = mybir.dt.float32
BF16 = mybir.dt.bfloat16
AF = mybir.ActivationFunctionType
ALU = mybir.AluOpType
AX = mybir.AxisListType

MAGIC = float(np.float32(1.5 * 2 ** 23))
TWO_PI = float(np.float32(2 * np.pi))
INV_2PI = float(np.float32(1.0 / (2 * np.pi)))
HALF_PI = float(np.float32(np.pi / 2))
NCORES = 8


class Cfg:
    def __init__(self, B=2, L=8192, D=2048, H=64, N=128, P=64, HID=8192,
                 ncores=NCORES, Dout=None):
        self.B, self.L, self.D, self.H, self.N, self.P, self.HID = B, L, D, H, N, P, HID
        self.DIN = H * P
        self.Dout = D if Dout is None else Dout
        self.ncores = ncores
        self.HPC = H // ncores
        self.Q = 128
        assert L % self.Q == 0 and D % 128 == 0 and N == 128 and P == 64
        self.NQ = L // self.Q
        self.Ltot = B * L
        self.DK = D // 128
        hp = self.HPC
        self.sec = {}
        off = 0
        for name, w in [("xp", hp * P), ("z", hp * P), ("B", hp * N), ("C", hp * N),
                        ("th", hp * (N // 2)), ("dt", hp), ("lam", hp)]:
            self.sec[name] = (off, w)
            off += w
        self.WROWS = off
        self.chunks = []
        for name in ["xp", "z", "B", "C", "th"]:
            o, w = self.sec[name]
            while w > 0:
                cw = min(512, w)
                self.chunks.append((o, cw))
                o += cw
                w -= cw
        self.chunks.append((self.sec["dt"][0], 2 * hp))
        self.NCH = len(self.chunks)
        self.SPAN = (B * L) // ncores


CFG = Cfg()
ENGS = ("T", "V", "S", "P", "Y")


class K:

    def __init__(self, nc):
        self.nc = nc
        self.ops = {e: [] for e in ENGS}
        self.semnames = []
        self.cnt = {}
        self.hwm = {e: {} for e in ENGS}
        self.sems = {}
        for s in ("T", "V", "S", "P", "dC", "dR", "dW", "dX0", "dX1", "dY0", "dY1", "dA", "cc"):
            self.new_sem(s)

    def new_sem(self, name):
        if name not in self.cnt:
            self.semnames.append(name)
            self.cnt[name] = 0

    def emit(self, eng, fn, waits=(), inc=None, n=1):
        ws = []
        h = self.hwm[eng]
        for wv in waits:
            if wv is None:
                continue
            s, v = wv
            if h.get(s, 0) < v:
                ws.append((s, v))
                h[s] = v
        tag = None
        if inc is not None:
            self.cnt[inc] += n
            tag = (inc, self.cnt[inc])

        def run(e, ws=tuple(ws), fn=fn, inc=inc, n=n):
            for s, v in ws:
                e.wait_ge(self.sems[s], v)
            inst = fn(e) if fn is not None else None
            if inc is not None:
                inst.then_inc(self.sems[inc], n)
        self.ops[eng].append(run)
        return tag

    def now(self, sem):
        return (sem, self.cnt[sem])

    def fence(self, tags, engines=ENGS):
        for e in engines:
            self.emit(e, None, waits=tags)

    def replay(self):
        nc = self.nc
        with contextlib.ExitStack() as st:
            for name in self.semnames:
                self.sems[name] = st.enter_context(nc.semaphore(name))
            with nc.Block() as block:
                def mk(engname):
                    def body(e):
                        for f in self.ops[engname]:
                            f(e)
                    return body
                block.tensor(mk("T"))
                block.vector(mk("V"))
                block.scalar(mk("S"))
                block.gpsimd(mk("P"))
                block.sync(mk("Y"))


def bcast_h(ap2d, hpc, w):
    return ap2d.unsqueeze(2).broadcast_to([ap2d.shape[0], hpc, w])


def bcast_rep(ap2d, hpc, w):
    return ap2d.unsqueeze(1).broadcast_to([ap2d.shape[0], hpc, w])


def v3(ap2d, hpc, w):
    return ap2d.rearrange("p (h w) -> p h w", h=hpc)



def emit_phaseA(k, cfg, io):
    nc = k.nc
    em = k.emit
    D, DK, Q, HPC, N, P = cfg.D, cfg.DK, cfg.Q, cfg.HPC, cfg.N, cfg.P
    WROWS = cfg.WROWS
    XW = HPC * P
    BW = HPC * N
    TW = HPC * (N // 2)
    n_inst = cfg.B * cfg.NQ

    c_ul = nc.inline_tensor(np.triu(np.ones((128, 128), np.float32)), "c_ul")
    c_astr = nc.inline_tensor(np.tril(np.ones((128, 128)), -1).astype(ml_dtypes.bfloat16), "c_astr")
    c_s1 = nc.inline_tensor(np.eye(128, k=-1, dtype=np.float32), "c_s1")
    c_m01 = nc.inline_tensor(
        np.triu(np.full((128, 128), -1e30)).astype(ml_dtypes.bfloat16), "c_m01")
    c_onesf = nc.inline_tensor(np.ones((128, 128), np.float32), "c_onesf")
    c_onesr = nc.inline_tensor(np.ones((1, 128), np.float32), "c_onesr")
    c_e127 = nc.inline_tensor(np.eye(128, dtype=np.float32)[127:128].copy(), "c_e127")
    c_idb = nc.inline_tensor(np.eye(128, dtype=ml_dtypes.bfloat16), "c_idb")
    cvals = np.array([1e-5, 1e-6, 1.0, HALF_PI], np.float32)
    c_sc = nc.inline_tensor(np.tile(cvals[None, :], (128, 1)), "c_sc")
    I_EPSLN, I_EPSRMS, I_ONE, I_HALFPI = 0, 1, 2, 3

    st = io["stack"]
    sb = lambda name, shape, dt: st.enter_context(nc.sbuf_tensor(name, shape, dt))
    psb = lambda name: st.enter_context(nc.psum_tensor(name, [128, 512], F32))

    ul = sb("ul", [128, 128], F32)
    astr = sb("astr", [128, 128], BF16)
    s1m = sb("s1m", [128, 128], F32)
    m01 = sb("m01", [128, 128], BF16)
    onesf = sb("onesf", [128, 128], F32)
    onesr = sb("onesr", [1, 128], F32)
    e127 = sb("e127", [1, 128], F32)
    idb = sb("idb", [128, 128], BF16)
    csc = sb("csc", [128, 4], F32)
    onescol = sb("onescol", [128, 1], F32)
    onescolb = sb("onescolb", [128, 1], BF16)
    wcs = sb("wcs", [128, DK * WROWS], BF16)
    wsum = sb("wsum", [1, WROWS], BF16)
    arep = sb("arep", [128, HPC], F32)
    wbrep = sb("wbrep", [128, N], BF16)
    wcrep = sb("wcrep", [128, N], BF16)
    bbrep = sb("bbrep", [128, HPC * N], BF16)
    cbrep = sb("cbrep", [128, HPC * N], BF16)
    hall = sb("hall", [128, HPC * P], F32)
    hallb = sb("hallb", [128, HPC * P], BF16)
    htmp = sb("htmp", [128, HPC * P], F32)
    carry = sb("carry", [1, TW], F32)
    zrow = sb("zrow", [1, 16], F32)
    rowtmp = sb("rowtmp", [1, 128], F32)

    def db(name, shape, dt):
        return [sb(name + str(i), shape, dt) for i in range(2)]

    xb = db("xb", [128, DK * 128], BF16)
    Xb = db("Xb", [128, XW], BF16)
    szb = db("szb", [128, XW], BF16)
    Bro = db("Bro", [128, BW], BF16)
    Cro = db("Cro", [128, BW], BF16)
    tiny = db("tiny", [128, 8 * HPC], F32)
    SL = {nm: idx * HPC for idx, nm in enumerate(["dt", "lam", "g", "gb", "s", "rdt", "c", "w"])}
    E8 = db("E8", [128, HPC], F32)
    scp = db("scp", [128, HPC], F32)
    web = db("web", [128, HPC], F32)
    yzT = db("yzT", [128, XW], BF16)

    xsq = sb("xsq", [128, DK * 128], BF16)
    st_m = sb("st_m", [1, 128], F32)
    st_q = sb("st_q", [1, 128], F32)
    st_r = [sb("st_r0", [1, 128], F32), sb("st_r1", [1, 128], F32)]
    st_rmn = [sb("st_rmn0", [1, 128], BF16), sb("st_rmn1", [1, 128], BF16)]
    rcolp = [sb("rcol0", [128, 1], F32), sb("rcol1", [128, 1], F32)]
    rsqcolp = [sb("rsqcol0", [128, 1], F32), sb("rsqcol1", [128, 1], F32)]
    rhcolp = [sb("rhcol0", [128, 1], F32), sb("rhcol1", [128, 1], F32)]
    eb = sb("eb", [128, XW], BF16)
    Braw = sb("Braw", [128, BW], BF16)
    Craw = sb("Craw", [128, BW], BF16)
    sqs = sb("sqs", [128, 1024], BF16)
    ropescr = sb("ropescr", [128, TW], BF16)
    q2B = sb("q2B", [128, HPC], F32)
    q2C = sb("q2C", [128, HPC], F32)
    rsB = sb("rsB", [128, HPC], F32)
    rsC = sb("rsC", [128, HPC], F32)
    sm1 = sb("sm1", [128, HPC], F32)
    sm2 = sb("sm2", [128, HPC], F32)
    sm3 = sb("sm3", [128, HPC], F32)
    sm4 = sb("sm4", [128, HPC], F32)
    lb = sb("lb", [128, HPC], F32)
    sth = sb("sth", [128, TW], F32)
    thk = sb("thk", [128, TW], F32)
    thp = sb("thp", [128, TW], F32)
    sinb = sb("sinb", [128, TW], BF16)
    cosb = sb("cosb", [128, TW], BF16)
    rhsall = sb("rhsall", [128, HPC * 128], BF16)
    expd = sb("expd", [128, HPC * 128], BF16)
    Bd = sb("Bd", [128, BW], BF16)
    Cd = sb("Cd", [128, BW], BF16)
    SWc = sb("SWc", [128, HPC * 128], BF16)
    SW = sb("SW", [128, HPC * 128], BF16)
    bcb = sb("bcb", [128, BW], BF16)
    sttb = sb("sttb", [128, HPC], F32)
    dsttb = sb("dsttb", [128, HPC], F32)
    ydg = sb("ydg", [128, XW], BF16)
    Bw = sb("Bw", [128, BW], BF16)
    ya = sb("ya", [128, XW], F32)
    yzb = sb("yzb", [128, XW], BF16)

    pj0, pj1 = psb("pj0"), psb("pj1")
    pcum, pell, psm, pdh, pa, pb = psb("pcum"), psb("pell"), psb("psm"), psb("pdh"), psb("pa"), psb("pb")
    PJ = [pj0, pj1]
    pdh_b = pdh.ap().bitcast(BF16)
    pa_b = pa.ap().bitcast(BF16)
    pb_b = pb.ap().bitcast(BF16)

    for dst, src in [(ul, c_ul), (astr, c_astr), (s1m, c_s1), (m01, c_m01),
                     (onesf, c_onesf), (onesr, c_onesr), (e127, c_e127),
                     (idb, c_idb), (csc, c_sc)]:
        em("Y", lambda e, d=dst, s=src: e.dma_start(out=d[:, :], in_=s.ap()),
           inc="dC", n=16)
    t_oc = em("V", lambda e: e.memset(onescol[:, :], 1.0), inc="V")
    em("V", lambda e: e.memset(onescolb[:, :], 1.0), inc="V")
    em("V", lambda e: e.memset(hall[:, :], 0.0), inc="V")
    em("V", lambda e: e.memset(hallb[:, :], 0.0), inc="V")
    em("V", lambda e: e.memset(carry[:, :], 0.0), inc="V")
    t_zr = em("V", lambda e: e.memset(zrow[:, :], 0.0), inc="V")

    nsplit = 4 if DK % 4 == 0 else 1
    kstep = DK // nsplit
    for part in range(nsplit):
        em("P", lambda e, part=part: e.dma_start(
            out=wcs[:, :].rearrange("p (k c) -> p k c", c=WROWS)[:, part * kstep:(part + 1) * kstep],
            in_=io["WcsT"].rearrange("(k p) c -> p k c", p=128)[:, part * kstep:(part + 1) * kstep]),
            inc="dW", n=16)
    t_wcs = k.now("dW")
    ld_row = sb("ld_row", [1, max(HPC * N, 128)], F32)
    row_free = [None]

    def ld_param(nm):
        t = em("Y", lambda e, nm=nm, w=int(io[nm].shape[1]): e.dma_start(
            out=ld_row[:, 0:w], in_=io[nm]), waits=[row_free[0]], inc="dR", n=16)
        return t
    t_const = k.now("dC")

    def ld_x(e, inst, par):
        return e.dma_start(
            out=xb[par][:, :].rearrange("p (k t) -> p k t", t=Q),
            in_=io["xT"].rearrange("(k p) t -> p k t", p=128)[:, :, inst * Q:(inst + 1) * Q])
    xload = {0: em("P", lambda e: ld_x(e, 0, 0), inc="dX0", n=16)}

    t_ld = ld_param("Alog")
    t1 = em("S", lambda e: e.activation(rowtmp[:, 0:HPC], ld_row[:, 0:HPC], AF.Exp),
            waits=[t_ld], inc="S")
    row_free[0] = t1
    t2 = em("V", lambda e: e.tensor_scalar(rowtmp[:, 0:HPC], rowtmp[:, 0:HPC], -1.0, None, ALU.mult),
            waits=[t1], inc="V")
    t3 = em("T", lambda e: e.matmul(psm[:, 0:HPC], onesr[:, :], rowtmp[:, 0:HPC],
                                    start=True, stop=True),
            waits=[t2, t_const], inc="T")
    t_arep = em("V", lambda e: e.tensor_copy(arep[:, :], psm[:, 0:HPC]), waits=[t3], inc="V")

    rep_prev = [t_arep]

    def rep_from_row(nm, width, dstbuf):
        t_ld = ld_param(nm)
        tlast = None
        for c in range((width + 511) // 512):
            w = min(512, width - 512 * c)
            tt = em("T", lambda e, c=c, w=w: e.matmul(
                psm[:, 0:w], onesr[:, :], ld_row[:, 512 * c:512 * c + w],
                start=True, stop=True),
                waits=[t_ld] + rep_prev, inc="T")
            tlast = em("V", lambda e, c=c, w=w, d=dstbuf: e.tensor_copy(
                d[:, 512 * c:512 * c + w], psm[:, 0:w]), waits=[tt], inc="V")
            rep_prev[:] = [tlast]
        row_free[0] = tt
        return tlast

    rep_from_row("Bnw", N, wbrep)
    rep_from_row("Cnw", N, wcrep)
    rep_from_row("Bb", HPC * N, bbrep)
    t_reps = rep_from_row("Cb", HPC * N, cbrep)
    t_par = t_reps

    ws_free = {}
    for ci, (co, cw) in enumerate(cfg.chunks):
        bank = PJ[ci % 2]
        for kk in range(DK):
            em("T", lambda e, kk=kk, co=co, cw=cw, bank=bank: e.matmul(
                bank[0:1, 0:cw], onescolb[:, :],
                wcs[:, kk * WROWS + co:kk * WROWS + co + cw],
                start=(kk == 0), stop=(kk == DK - 1)),
                waits=([t_wcs, t_oc, t_reps, ws_free.get(ci - 2)]) if kk == 0 else [],
                inc="T" if kk == DK - 1 else None)
        tt = k.now("T")
        ws_free[ci] = em("V", lambda e, co=co, cw=cw, bank=bank: e.tensor_copy(
            wsum[:, co:co + cw], bank[0:1, 0:cw]), waits=[tt], inc="V")
    t_wsum = ws_free[cfg.NCH - 1]

    def sec_chunks(name):
        o, w = cfg.sec[name]
        out = []
        for ci2, (co2, cw2) in enumerate(cfg.chunks):
            if co2 >= o and co2 < o + w:
                out.append((ci2, co2 - o, cw2))
        return out

    end_prev = [k.now(s) for s in ("T", "V", "S", "P")]

    store_tags = {}
    st = {}

    def emit_stats(i2, prev_l2):
        p2 = i2 % 2
        t_sq = em("S", lambda e: e.activation(xsq[:, :], xb[p2][:, :], AF.Square),
                  waits=[xload[i2], st.get(i2 - 1, {}).get("T")], inc="S")
        for kk in range(DK):
            em("T", lambda e, kk=kk: e.matmul(
                psm[0:1, 0:128], onescolb[:, :], xb[p2][:, kk * 128:(kk + 1) * 128],
                start=(kk == 0), stop=False),
                waits=[xload[i2], t_wsum, st.get(i2 - 1, {}).get("rc")] if kk == 0 else [])
        for kk in range(DK):
            em("T", lambda e, kk=kk: e.matmul(
                psm[0:1, 128:256], onescolb[:, :], xsq[:, kk * 128:(kk + 1) * 128],
                start=False, stop=(kk == DK - 1)),
                waits=[t_sq] if kk == 0 else [],
                inc="T" if kk == DK - 1 else None)
        t_stats = k.now("T")
        t_r1 = em("V", lambda e: e.tensor_scalar(st_m[:, :], psm[0:1, 0:128],
                                                 1.0 / cfg.D, None, ALU.mult),
                  waits=[t_stats], inc="V")
        t_r2 = em("V", lambda e: e.tensor_scalar(st_q[:, :], psm[0:1, 128:256],
                                                 1.0 / cfg.D, None, ALU.mult),
                  waits=[t_stats], inc="V")
        t_r3 = em("V", lambda e: e.tensor_tensor(st_r[p2][:, :], st_m[:, :], st_m[:, :], ALU.mult),
                  waits=[t_r1], inc="V")
        t_r4 = em("V", lambda e: e.tensor_tensor(st_r[p2][:, :], st_q[:, :], st_r[p2][:, :], ALU.subtract),
                  waits=[t_r2, t_r3], inc="V")
        t_r5 = em("S", lambda e: e.activation(st_r[p2][:, :], st_r[p2][:, :], AF.Ln,
                                              bias=csc[0:1, I_EPSLN:I_EPSLN + 1]),
                  waits=[t_r4], inc="S")
        t_r6 = em("S", lambda e: e.activation(st_r[p2][:, :], st_r[p2][:, :], AF.Exp, scale=-0.5),
                  waits=[t_r5], inc="S")
        t_r7 = em("V", lambda e: e.tensor_scalar(st_rmn[p2][:, :], st_m[:, :], -1.0, None, ALU.mult),
                  waits=[t_r1], inc="V")
        t_rc1 = em("T", lambda e: e.matmul(psm[:, 256:257], st_r[p2][:, :], onesr[:, 0:1],
                                           start=True, stop=True),
                   waits=[t_r6], inc="T")
        t_rc = em("V", lambda e: e.tensor_copy(rcolp[p2][:, :], psm[:, 256:257]),
                  waits=[t_rc1, prev_l2], inc="V")
        t_rsq = em("V", lambda e: e.tensor_tensor(rsqcolp[p2][:, :], rcolp[p2][:, :],
                                                  rcolp[p2][:, :], ALU.mult),
                   waits=[t_rc], inc="V")
        t_rh = em("V", lambda e: e.tensor_scalar(rhcolp[p2][:, :], rcolp[p2][:, :],
                                                 0.5, None, ALU.mult),
                  waits=[t_rc], inc="V")
        st[i2] = dict(T=t_stats, sq=t_sq, rc=t_rc, rsq=t_rsq, rh=t_rh, r7=t_r7)

    pv = {}
    for it in range(n_inst + 1):
        i, j = it, it - 1
        pi, pjy = i % 2, j % 2
        have_i = i < n_inst
        have_j = j >= 0
        if it == 0 or not have_i:
            k.fence(end_prev)

        if have_j:
            tj = lambda nm: tiny[pjy][:, SL[nm]:SL[nm] + HPC]
            hpb = (HPC + 1) // 2
            t_rhs = em("V", lambda e, a=bcast_h(tj("s"), HPC, 128): e.tensor_tensor(
                v3(rhsall[:, :], HPC, 128), a,
                bcast_rep(ul[:, :], HPC, 128), ALU.mult),
                waits=[pv.get("dm")], inc="V")

        if have_i:
            b_i, q_i = divmod(i, cfg.NQ)
            if it == 0:
                emit_stats(0, None)
            sti = st[i]
            rcol, rsqcol, rhcol = rcolp[pi], rsqcolp[pi], rhcolp[pi]
            t_rc, t_rsq, t_rh, t_r7 = sti["rc"], sti["rsq"], sti["rh"], sti["r7"]
            if i + 1 < n_inst:
                xload[i + 1] = em("P", lambda e, i=i: ld_x(e, i + 1, (i + 1) % 2),
                                  waits=[pv.get("projlast"), st.get(i - 1, {}).get("sq")],
                                  inc="dX%d" % ((i + 1) % 2), n=16)
            proj = {}
            bank_free = {}
            for ci, (co, cw) in enumerate(cfg.chunks):
                bank = PJ[ci % 2]
                w0 = [xload[i]]
                if ci == 0:
                    w0.append(pv.get("sth"))
                elif ci == 1:
                    w0 += [pv.get("u1"), pv.get("l2")]
                if ci - 2 >= 0:
                    w0.append(bank_free.get(ci - 2))
                for kk in range(DK):
                    em("T", lambda e, kk=kk, co=co, cw=cw, bank=bank, pi=pi: e.matmul(
                        bank[:, 0:cw], xb[pi][:, kk * 128:(kk + 1) * 128],
                        wcs[:, kk * WROWS + co:kk * WROWS + co + cw],
                        start=(kk == 0), stop=False),
                        waits=w0 if kk == 0 else [])
                em("T", lambda e, co=co, cw=cw, bank=bank, pi=pi: e.matmul(
                    bank[:, 0:cw], st_rmn[pi][:, :], wsum[:, co:co + cw],
                    start=False, stop=True), waits=[t_r7], inc="T")
                proj[ci] = k.now("T")

                name = None
                for nm in ["xp", "z", "B", "C", "th", "dt"]:
                    o, w = cfg.sec[nm]
                    if o <= co < o + w:
                        name = nm
                        so = co - o
                        break
                if name == "xp":
                    t = em("S", lambda e, bank=bank, so=so, cw=cw, pi=pi, rcol=rcol: e.activation(
                        Xb[pi][:, so:so + cw], bank[:, 0:cw], AF.Identity,
                        scale=rcol[:, :]), waits=[proj[ci], t_rc], inc="S")
                    bank_free[ci] = t
                elif name == "z":
                    t_zcp = em("V", lambda e, bank=bank, so=so, cw=cw: e.tensor_copy(
                        ropescr[:, so:so + cw], bank[:, 0:cw]),
                        waits=[proj[ci], pv.get("rope")], inc="V")
                    z_so, z_cw = so, cw
                    bank_free[ci] = t_zcp
                elif name in ("B", "C"):
                    raw = Braw if name == "B" else Craw
                    t_cp = em("V", lambda e, bank=bank, so=so, cw=cw, raw=raw: e.tensor_copy(
                        raw[:, so:so + cw], bank[:, 0:cw]),
                        waits=[proj[ci], pv.get("rope")], inc="V")
                    bank_free[ci] = t_cp
                elif name == "th":
                    bank_free[ci] = None
                    th_bank = bank
                    t_thp = proj[ci]
                elif name == "dt":
                    bank_free[ci] = None
                    dt_bank = bank
                    t_dtp = proj[ci]

            def bc_chain(name, raw, q2, rs, wrep, brep, outb):
                hlist = sec_chunks(name)
                tq = []
                for (ci2, so, cw2) in hlist:
                    nh = cw2 // N
                    h0 = so // N
                    ts1 = em("S", lambda e, so=so, cw2=cw2, raw=raw: e.activation(
                        sqs[:, so:so + cw2], raw[:, so:so + cw2], AF.Square),
                        waits=[bank_free[ci2]], inc="S")
                    tq.append(em("V", lambda e, so=so, cw2=cw2, nh=nh, h0=h0, q2=q2: e.tensor_reduce(
                        q2[:, h0:h0 + nh].unsqueeze(2),
                        sqs[:, so:so + cw2].rearrange("p (h n) -> p h n", h=nh),
                        AX.X, ALU.add), waits=[ts1], inc="V"))
                t_ms = em("V", lambda e, q2=q2, rsqcol=rsqcol: e.tensor_scalar(
                    q2[:, :], q2[:, :], rsqcol[:, :], 1.0 / N, ALU.mult, ALU.mult),
                    waits=tq + [t_rsq], inc="V")
                t_l = em("S", lambda e, q2=q2: e.activation(
                    q2[:, :], q2[:, :], AF.Ln, bias=csc[:, I_EPSRMS:I_EPSRMS + 1]),
                    waits=[t_ms], inc="S")
                t_e2 = em("S", lambda e, q2=q2: e.activation(
                    q2[:, :], q2[:, :], AF.Exp, scale=-0.5), waits=[t_l], inc="S")
                t_rs = em("V", lambda e, q2=q2, rs=rs, rcol=rcol: e.tensor_scalar(
                    rs[:, :], q2[:, :], rcol[:, :], None, ALU.mult), waits=[t_e2], inc="V")
                def emit_value(extra):
                    w1 = [t_rs] + [bank_free[c2] for (c2, _, _) in hlist] + extra
                    t_v1 = []
                    for h in range(HPC):
                        t_v1.append(em("S", lambda e, h=h, raw=raw, rs=rs: e.activation(
                            raw[:, h * N:(h + 1) * N], raw[:, h * N:(h + 1) * N],
                            AF.Identity, scale=rs[:, h:h + 1]),
                            waits=w1 if h == 0 else [], inc="S"))
                    t_v2 = em("V", lambda e, wrep=wrep: e.tensor_tensor(
                        v3(outb[:, :], HPC, N), v3(raw[:, :], HPC, N),
                        bcast_rep(wrep[:, :], HPC, N), ALU.mult), waits=t_v1, inc="V")
                    t_v3 = em("V", lambda e, brep=brep: e.tensor_tensor(
                        outb[:, :], outb[:, :], brep[:, :], ALU.add), waits=[t_v2], inc="V")
                    return t_v3
                return emit_value

            fB = bc_chain("B", Braw, q2B, rsB, wbrep, bbrep, Braw)
            fC = bc_chain("C", Craw, q2C, rsC, wcrep, cbrep, Craw)

            dcol = lambda nm: tiny[pi][:, SL[nm]:SL[nm] + HPC]
            t_u1 = em("V", lambda e, db_=dt_bank, rcol=rcol: e.tensor_scalar(
                sm1[:, :], db_[:, 0:HPC], rcol[:, :], None, ALU.mult),
                waits=[t_dtp, t_rc, pv.get("we")], inc="V")
            t_u2 = em("S", lambda e: e.activation(sm2[:, :], sm1[:, :], AF.Abs), waits=[t_u1], inc="S")
            t_u3 = em("S", lambda e: e.activation(sm2[:, :], sm2[:, :], AF.Exp, scale=-1.0),
                      waits=[t_u2], inc="S")
            t_u4 = em("S", lambda e: e.activation(sm2[:, :], sm2[:, :], AF.Ln,
                                                  bias=csc[:, I_ONE:I_ONE + 1]),
                      waits=[t_u3], inc="S")
            t_u5 = em("V", lambda e: e.tensor_scalar(sm4[:, :], sm1[:, :], 0.0, None, ALU.max),
                      waits=[t_u1], inc="V")
            t_dt = em("V", lambda e, a=dcol("dt"): e.tensor_tensor(a, sm4[:, :], sm2[:, :], ALU.add),
                      waits=[t_u4, t_u5], inc="V")
            t_rdt = em("V", lambda e, a=dcol("rdt"), b=dcol("dt"), rcol=rcol: e.tensor_scalar(a, b, rcol[:, :], None, ALU.mult),
                       waits=[t_dt], inc="V")
            t_sth = em("V", lambda e, a=bcast_h(dcol("rdt"), HPC, N // 2), tb=th_bank: e.tensor_tensor(
                v3(sth[:, :], HPC, N // 2), v3(tb[:, 0:TW], HPC, N // 2),
                a, ALU.mult),
                waits=[t_thp, t_rdt], inc="V")
            t_l2 = em("S", lambda e, db_=dt_bank, rhcol=rhcol: e.activation(
                sm3[:, :], db_[:, HPC:2 * HPC], AF.Tanh, scale=rhcol[:, :]),
                waits=[t_dtp, t_rh], inc="S")
            t_lam = em("V", lambda e, a=dcol("lam"): e.tensor_scalar(
                a, sm3[:, :], 0.5, 0.5, ALU.mult, ALU.add), waits=[t_l2], inc="V")
            t_e = em("S", lambda e, so=z_so, cw=z_cw, rhcol=rhcol: e.activation(
                eb[:, so:so + cw], ropescr[:, so:so + cw], AF.Tanh, scale=rhcol[:, :]),
                waits=[t_zcp, t_rh], inc="S")
            t_zh = em("V", lambda e, so=z_so, cw=z_cw, rhcol=rhcol: e.tensor_scalar(
                thp[:, so:so + cw], ropescr[:, so:so + cw], rhcol[:, :], None, ALU.mult),
                waits=[t_zcp, t_rh], inc="V")
            t_z = em("V", lambda e, so=z_so, cw=z_cw, pi=pi: e.scalar_tensor_tensor(
                szb[pi][:, so:so + cw], eb[:, so:so + cw], 1.0,
                thp[:, so:so + cw], ALU.add, ALU.mult),
                waits=[t_e, t_zh], inc="V")
            t_g = em("V", lambda e, a=dcol("g"), b=dcol("dt"), c=dcol("lam"): e.tensor_tensor(a, b, c, ALU.mult),
                     waits=[t_dt, t_lam], inc="V")
            t_gb = em("V", lambda e, a=dcol("gb"), b=dcol("dt"), c=dcol("g"): e.tensor_tensor(a, b, c, ALU.subtract),
                      waits=[t_g], inc="V")
            t_s = em("V", lambda e, a=dcol("s"), b=dcol("dt"): e.tensor_tensor(a, b, arep[:, :], ALU.mult),
                     waits=[t_dt, t_arep], inc="V")

        if have_j:
            t_dm = []
            for h in range(HPC):
                bank = pa if h < hpb else pb
                off = (h % hpb) * 128
                last = (h % hpb == hpb - 1 or h == HPC - 1)
                em("T", lambda e, h=h, bank=bank, off=off: e.matmul(
                    bank[:, off:off + 128], astr[:, :],
                    rhsall[:, h * 128:(h + 1) * 128],
                    start=(h % hpb == 0), stop=False),
                    waits=[t_rhs, pv.get("yb")] if h == 0 else [])
                em("T", lambda e, h=h, bank=bank, off=off, last=last: e.matmul(
                    bank[:, off:off + 128], m01[:, :], idb[:, :],
                    start=False, stop=last),
                    inc="T" if last else None)
                if last:
                    t_dm.append(k.now("T"))
            t_x1 = em("S", lambda e: e.activation(expd[:, 0:hpb * 128],
                                                  pa[:, 0:hpb * 128], AF.Exp),
                      waits=[t_dm[0]], inc="S")
            t_x2 = em("S", lambda e: e.activation(expd[:, hpb * 128:HPC * 128],
                                                  pb[:, 0:(HPC - hpb) * 128], AF.Exp),
                      waits=[t_dm[-1]], inc="S")

            for h in range(HPC):
                bank = pa_b if h < hpb else pb_b
                off = (h % hpb) * 256
                em("T", lambda e, h=h, bank=bank, off=off, pjy=pjy: e.transpose(
                    bank[:, off:off + 128], Bro[pjy][:, h * N:h * N + 128], idb[:, :]),
                    waits=[t_x1, t_x2, pv.get("rope")] if h == 0 else [])
                em("T", lambda e, h=h, bank=bank, off=off, pjy=pjy: e.transpose(
                    bank[:, off + 128:off + 256], Cro[pjy][:, h * N:h * N + 128], idb[:, :]),
                    inc="T" if h == HPC - 1 else None)
            t_tr = k.now("T")
            t_bd1 = em("V", lambda e: e.tensor_copy(
                v3(Bd[:, 0:hpb * 128], hpb, 128),
                pa_b[:, 0:hpb * 256].rearrange("p (h u n) -> p u h n", u=2, n=128)[:, 0]),
                waits=[t_tr], inc="V")
            t_bd2 = em("V", lambda e: e.tensor_copy(
                v3(Bd[:, hpb * 128:HPC * 128], HPC - hpb, 128),
                pb_b[:, 0:(HPC - hpb) * 256].rearrange("p (h u n) -> p u h n", u=2, n=128)[:, 0]),
                waits=[t_tr], inc="V")
            t_cd1 = em("V", lambda e: e.tensor_copy(
                v3(Cd[:, 0:hpb * 128], hpb, 128),
                pa_b[:, 0:hpb * 256].rearrange("p (h u n) -> p u h n", u=2, n=128)[:, 1]),
                waits=[t_tr], inc="V")
            t_cd2 = em("V", lambda e: e.tensor_copy(
                v3(Cd[:, hpb * 128:HPC * 128], HPC - hpb, 128),
                pb_b[:, 0:(HPC - hpb) * 256].rearrange("p (h u n) -> p u h n", u=2, n=128)[:, 1]),
                waits=[t_tr], inc="V")
            t_bcd = [t_bd1, t_bd2, t_cd1, t_cd2]


        if have_i:
            em("T", lambda e: e.matmul(pcum[:, 0:TW], ul[:, :], sth[:, 0:TW],
                                       start=True, stop=False),
               waits=[t_sth, t_const, pv.get("th"), pv.get("cu")])
            t_cum = em("T", lambda e: e.matmul(pcum[:, 0:TW], onesr[:, :], carry[:, 0:TW],
                                               start=False, stop=True), inc="T")
            t_tot = em("T", lambda e: e.matmul(pdh[0:1, 0:TW], onesf[:, 0:1], sth[:, 0:TW],
                                               start=True, stop=True),
                       waits=[pv.get("ycp")], inc="T")
            t_ell = em("T", lambda e, a=dcol("s"): e.matmul(pell[:, 0:HPC], ul[:, :], a,
                                               start=True, stop=True),
                       waits=[t_s, pv.get("c"), pv.get("sc")], inc="T")
            t_st2 = em("T", lambda e, a=dcol("s"): e.matmul(pell[:, HPC:2 * HPC], onesf[:, :], a,
                                               start=True, stop=True), inc="T")
            if q_i == cfg.NQ - 1:
                t_cu = em("V", lambda e: e.memset(carry[:, :], 0.0), waits=[t_cum], inc="V")
            else:
                t_cu = em("V", lambda e: e.tensor_tensor(
                    carry[:, 0:TW], carry[:, 0:TW], pdh[0:1, 0:TW], ALU.add),
                    waits=[t_tot, t_cum], inc="V")
            t_lb = em("V", lambda e: e.tensor_copy(lb[:, :], pell[:, 0:HPC]), waits=[t_st2], inc="V")
            t_E8 = em("S", lambda e, pi=pi: e.activation(E8[pi][:, :], pell[:, 0:HPC], AF.Exp),
                      waits=[t_st2, pv.get("h3")], inc="S")
            t_sc = em("S", lambda e, pi=pi: e.activation(scp[pi][:, :], pell[:, HPC:2 * HPC], AF.Exp),
                      waits=[t_st2, pv.get("h3")], inc="S")
            t_d1 = em("V", lambda e: e.tensor_tensor(sm1[:, :], pell[:, HPC:2 * HPC], lb[:, :],
                                                     ALU.subtract),
                      waits=[t_st2, t_lb], inc="V")
            t_we = em("S", lambda e, pi=pi: e.activation(web[pi][:, :], sm1[:, :], AF.Exp),
                      waits=[t_d1], inc="S")

            t_k1 = em("V", lambda e: e.tensor_scalar(thk[:, :], pcum[:, 0:TW],
                                                     INV_2PI, MAGIC, ALU.mult, ALU.add),
                      waits=[t_cum, pv.get("cos")], inc="V")
            t_k2 = em("V", lambda e: e.tensor_scalar(thk[:, :], thk[:, :], MAGIC, None, ALU.subtract),
                      waits=[t_k1], inc="V")
            t_th = em("V", lambda e: e.scalar_tensor_tensor(
                thp[:, :], thk[:, :], -TWO_PI, pcum[:, 0:TW], ALU.mult, ALU.add),
                waits=[t_k2], inc="V")
            t_sin = em("S", lambda e: e.activation(sinb[:, :], thp[:, :], AF.Sin),
                       waits=[t_th, pv.get("rope")], inc="S")
            t_ab = em("S", lambda e: e.activation(thk[:, :], thp[:, :], AF.Abs), waits=[t_th], inc="S")
            t_cos = em("S", lambda e: e.activation(cosb[:, :], thk[:, :], AF.Sin,
                                                   scale=-1.0, bias=csc[:, I_HALFPI:I_HALFPI + 1]),
                       waits=[t_ab], inc="S")

            def rope2(src, dst, t_src, par, prev):
                sv = lambda b, h0, h1: b[:, :].rearrange("p (h n) -> p h n", h=HPC)[:, :, h0:h1]
                cs = v3(cosb[:, :], HPC, N // 2)
                sn = v3(sinb[:, :], HPC, N // 2)
                scr = v3(ropescr[:, :], HPC, N // 2)
                t1 = em("V", lambda e: e.tensor_tensor(scr, sv(src, N // 2, N), sn, ALU.mult),
                        waits=[t_src, t_sin, prev, t_z, pv.get("tr"), pv.get("bw")], inc="V")
                t2 = em("V", lambda e: e.tensor_tensor(sv(dst[par], 0, N // 2),
                                                       sv(src, 0, N // 2), cs, ALU.mult),
                        waits=[t_cos], inc="V")
                t3 = em("V", lambda e: e.tensor_tensor(sv(dst[par], 0, N // 2),
                                                       sv(dst[par], 0, N // 2), scr, ALU.subtract),
                        waits=[t1, t2], inc="V")
                t4 = em("V", lambda e: e.tensor_tensor(scr, sv(src, N // 2, N), cs, ALU.mult),
                        waits=[t3], inc="V")
                t5 = em("V", lambda e: e.tensor_tensor(sv(dst[par], N // 2, N),
                                                       sv(src, 0, N // 2), sn, ALU.mult),
                        inc="V")
                t6 = em("V", lambda e: e.tensor_tensor(sv(dst[par], N // 2, N),
                                                       sv(dst[par], N // 2, N), scr, ALU.add),
                        waits=[t4, t5], inc="V")
                return t6


        if have_j:
            b_j, q_j = divmod(j, cfg.NQ)

            for h in range(HPC):
                bank = pa if h < hpb else pb
                off = (h % hpb) * 128
                em("T", lambda e, h=h, bank=bank, off=off: e.matmul(
                    bank[:, off:off + 128], Bd[:, h * 128:(h + 1) * 128],
                    Cd[:, h * 128:(h + 1) * 128], start=True, stop=True),
                    waits=t_bcd if h == 0 else [],
                    inc="T" if (h == hpb - 1 or h == HPC - 1) else None)
            t_st = k.now("T")

            gnext_row = (tiny[pi][0:1, SL["gb"]:SL["gb"] + HPC] if have_i
                         else zrow[0:1, 0:HPC])
            cm_w = [t_zr, pv.get("c")]
            if have_i:
                cm_w += [t_gb, t_d1, t_sc]
            em("T", lambda e, a=tj("gb"): e.matmul(pell[:, 2 * HPC:3 * HPC], s1m[:, :], a,
                                       start=True, stop=False), waits=cm_w)
            t_cm = em("T", lambda e, g=gnext_row: e.matmul(
                pell[:, 2 * HPC:3 * HPC], e127[:, :], g, start=False, stop=True), inc="T")
            t_c = em("V", lambda e, a=tj("c"), b=tj("g"): e.tensor_tensor(a, pell[:, 2 * HPC:3 * HPC],
                                                    b, ALU.add), waits=[t_cm], inc="V")
            t_w = em("V", lambda e, a=tj("w"), b=tj("c"), pjy=pjy: e.tensor_tensor(a, web[pjy][:, :], b, ALU.mult),
                     waits=[t_c], inc="V")

            t_swc1 = em("V", lambda e, a=bcast_h(tj("c")[:, 0:hpb], hpb, 128): e.tensor_tensor(
                v3(SWc[:, 0:hpb * 128], hpb, 128),
                pa[:, 0:hpb * 128].rearrange("p (h n) -> p h n", h=hpb),
                a, ALU.mult),
                waits=[t_st, t_c], inc="V")
            t_swc2 = em("V", lambda e, a=bcast_h(tj("c")[:, hpb:HPC], HPC - hpb, 128): e.tensor_tensor(
                v3(SWc[:, hpb * 128:HPC * 128], HPC - hpb, 128),
                pb[:, 0:(HPC - hpb) * 128].rearrange("p (h n) -> p h n", h=HPC - hpb),
                a, ALU.mult),
                waits=[t_st], inc="V")
            t_sw = em("V", lambda e: e.tensor_tensor(SW[:, :], SWc[:, :], expd[:, :], ALU.mult),
                      waits=[t_swc1, t_swc2, t_x1, t_x2], inc="V")

            t_bc = em("V", lambda e, pjy=pjy: e.tensor_tensor(
                bcb[:, :], Bro[pjy][:, :], Cro[pjy][:, :], ALU.mult),
                waits=[pv.get("rope")], inc="V")
            t_stt = em("V", lambda e: e.tensor_reduce(
                sttb[:, :].unsqueeze(2), v3(bcb[:, :], HPC, N), AX.X, ALU.add),
                waits=[t_bc], inc="V")
            t_dstt = em("V", lambda e, a=tj("g"): e.tensor_tensor(dsttb[:, :], a, sttb[:, :],
                                                       ALU.mult), waits=[t_stt], inc="V")
            t_ydg = em("V", lambda e, pjy=pjy: e.tensor_tensor(
                v3(ydg[:, :], HPC, P), v3(Xb[pjy][:, :], HPC, P),
                bcast_h(dsttb[:, :], HPC, P), ALU.mult), waits=[t_dstt], inc="V")

            t_bw = em("V", lambda e, pjy=pjy, a=bcast_h(tj("w"), HPC, N): e.tensor_tensor(
                v3(Bw[:, :], HPC, N), v3(Bro[pjy][:, :], HPC, N),
                a, ALU.mult), waits=[t_w], inc="V")

            for h in range(HPC):
                em("T", lambda e, h=h, pjy=pjy: e.matmul(
                    pa[:, h * P:(h + 1) * P], SW[:, h * 128:(h + 1) * 128],
                    Xb[pjy][:, h * P:(h + 1) * P], start=True, stop=True),
                    waits=[t_sw, t_swc1] if h == 0 else [],
                    inc="T" if h == HPC - 1 else None)
            t_yi = k.now("T")
            for h in range(HPC):
                em("T", lambda e, h=h: e.matmul(
                    pb[:, h * P:(h + 1) * P], Cd[:, h * 128:(h + 1) * 128],
                    hallb[:, h * P:(h + 1) * P], start=True, stop=True),
                    waits=[t_swc2, t_sw, pv.get("h3")] if h == 0 else [],
                    inc="T" if h == HPC - 1 else None)
            t_ye = k.now("T")
            for h in range(HPC):
                em("T", lambda e, h=h, pjy=pjy: e.matmul(
                    pdh[:, h * P:(h + 1) * P], Bw[:, h * N:(h + 1) * N],
                    Xb[pjy][:, h * P:(h + 1) * P], start=True, stop=True),
                    waits=[t_bw, pv.get("ycp")] if h == 0 else [],
                    inc="T" if h == HPC - 1 else None)
            t_dh = k.now("T")

            t_ya = em("V", lambda e, pjy=pjy: e.tensor_tensor(
                v3(ya[:, :], HPC, P), pa[:, 0:XW].rearrange("p (h n) -> p h n", h=HPC)
                if False else pb[:, 0:XW].rearrange("p (h n) -> p h n", h=HPC),
                bcast_h(E8[pjy][:, :], HPC, P), ALU.mult), waits=[t_ye], inc="V")
            t_yb = em("V", lambda e: e.tensor_tensor(ya[:, :], ya[:, :], pa[:, 0:XW], ALU.add),
                      waits=[t_ya, t_yi], inc="V")
            t_yc = em("V", lambda e: e.tensor_tensor(ya[:, :], ya[:, :], ydg[:, :], ALU.add),
                      waits=[t_yb, t_ydg], inc="V")
            t_yz = em("V", lambda e, pjy=pjy: e.tensor_tensor(
                yzb[:, :], ya[:, :], szb[pjy][:, :], ALU.mult), waits=[t_yc], inc="V")

            t_h1 = em("V", lambda e, pjy=pjy: e.tensor_tensor(
                v3(htmp[:, :], HPC, P), v3(hall[:, :], HPC, P),
                bcast_h(scp[pjy][:, :], HPC, P), ALU.mult), waits=[t_ye, t_dh], inc="V")
            if q_j == cfg.NQ - 1:
                t_h2 = em("V", lambda e: e.memset(hall[:, :], 0.0), waits=[t_h1], inc="V")
            else:
                t_h2 = em("V", lambda e: e.tensor_tensor(hall[:, :], htmp[:, :],
                                                         pdh[:, 0:XW], ALU.add),
                          waits=[t_h1], inc="V")
            t_h3 = em("V", lambda e: e.tensor_copy(hallb[:, :], hall[:, :]),
                      waits=[t_h2], inc="V")

            ntile = XW // 128
            for ti in range(ntile):
                em("T", lambda e, ti=ti: e.transpose(
                    pdh_b[:, ti * 128:(ti + 1) * 128], yzb[:, ti * 128:(ti + 1) * 128],
                    idb[:, :]),
                    waits=[t_yz, t_h2] if ti == 0 else [],
                    inc="T" if ti == ntile - 1 else None)
            t_yt = k.now("T")
            t_ycp = em("V", lambda e, pjy=pjy: e.tensor_copy(
                yzT[pjy][:, :], pdh_b[:, 0:XW]),
                waits=[t_yt, store_tags.get(j - 2)], inc="V")
            if io.get("yzS_cm") is not None:
                ci, lc = (j * Q) // io["LCH"], (j * Q) % io["LCH"]
                store_tags[j] = em("Y", lambda e, ci=ci, lc=lc, pjy=pjy: e.dma_start(
                    out=io["yzS_cm"][ci * XW:(ci + 1) * XW, lc:lc + Q]
                        .rearrange("(g p) t -> p g t", p=128),
                    in_=yzT[pjy][:, :].rearrange("p (g t) -> p g t", t=Q)),
                    waits=[t_ycp], inc="dY%d" % pjy, n=16)
            else:
                store_tags[j] = em("Y", lambda e, j=j, pjy=pjy: e.dma_start(
                    out=io["yzS"].rearrange("(g p) t -> p g t", p=128)[:, :, j * Q:(j + 1) * Q],
                    in_=yzT[pjy][:, :].rearrange("p (g t) -> p g t", t=Q)),
                    waits=[t_ycp], inc="dY%d" % pjy, n=16)
            if io.get("yz_cb") is not None:
                io["yz_cb"](j, store_tags)

        if have_i and i + 1 < n_inst:
            emit_stats(i + 1, pv.get("l2"))

        if have_i:
            t_bv = fB([])
            t_cv = fC([])
            t_brope = rope2(Braw, Bro, t_bv, pi, t_cv)
            t_crope = rope2(Craw, Cro, t_cv, pi, t_brope)

        if have_i:
            pv.update(projlast=proj[cfg.NCH - 1],
                      sth=t_sth, u1=t_u1, l2=t_l2, th=t_th, cu=t_cu,
                      cos=t_cos, rope=t_crope, we=t_we, sc=t_sc, c=t_d1,
                      d1=t_d1)
        if have_j:
            pv.update(c=t_c, dm=t_dm[-1], yb=t_yb, ycp=t_ycp, h3=t_h3,
                      tr=t_tr, bw=t_bw)
        end_prev = [k.now(s) for s in ("T", "V", "S", "P")]

    return [k.now("dY0"), k.now("dY1")]


def build_phaseA(cfg):
    nc = bass.Bass("TRN2", target_bir_lowering=False)
    k = K(nc)
    io = {}
    io["xT"] = nc.dram_tensor("xT", [cfg.D, cfg.Ltot], F32, kind="ExternalInput").ap()
    io["WcsT"] = nc.dram_tensor("WcsT", [cfg.D, cfg.WROWS], F32, kind="ExternalInput").ap()
    for nm, shape in [("Alog", [1, cfg.HPC]), ("Bnw", [1, cfg.N]), ("Cnw", [1, cfg.N]),
                      ("Bb", [1, cfg.HPC * cfg.N]), ("Cb", [1, cfg.HPC * cfg.N])]:
        io[nm] = nc.dram_tensor(nm, shape, F32, kind="ExternalInput").ap()
    io["yzS"] = nc.dram_tensor("yzS", [cfg.HPC * cfg.P, cfg.Ltot], BF16,
                               kind="ExternalOutput").ap()
    with contextlib.ExitStack() as stack:
        io["stack"] = stack
        t_done = emit_phaseA(k, cfg, io)
        k.emit("Y", None, waits=t_done)
        k.replay()
    return nc



def emit_phaseW(k, cfg, io):
    nc = k.nc
    em = k.emit
    HID = cfg.HID
    RROWS = cfg.Dout // cfg.ncores
    st = io["stack"]
    sb = lambda name, shape, dt: st.enter_context(nc.sbuf_tensor(name, shape, dt))
    wA = sb("wA", [128, HID], F32)
    wB = sb("wB", [128, HID], F32)
    wbf = sb("wbf", [128, HID], BF16)
    idwb = sb("idwb", [128, 128], BF16)
    tout = sb("tout", [128, 128], BF16)
    pw0 = st.enter_context(nc.psum_tensor("pw0", [128, 512], F32))
    pw0b = pw0.ap().bitcast(BF16)
    c_idw = nc.inline_tensor(np.eye(128, dtype=ml_dtypes.bfloat16), "c_idw")
    em("Y", lambda e: e.dma_start(out=idwb[:, :], in_=c_idw.ap()), inc="dC", n=16)
    t_id = k.now("dC")
    k.new_sem("dWF")
    k.new_sem("dWS")
    scale = 1.0 / math.sqrt(HID)
    ntile = (RROWS + 127) // 128
    prev_bf = None
    prev_cp = None
    prev_pv = None
    for tile in range(ntile):
        p0 = tile * 128
        pn = min(128, RROWS - p0)
        tl = em("Y", lambda e, p0=p0, pn=pn: e.dma_start(
            out=wA[0:pn, :], in_=io["W2s"][p0:p0 + pn, :]),
            waits=[prev_bf], inc="dWF", n=16)
        bufs = [wA, wB]
        t_st = tl
        nst = int(math.log2(HID))
        for si in range(nst):
            h = 1 << si
            src, dst = bufs[si % 2], bufs[(si + 1) % 2]
            va = lambda b, h=h, pn=pn: b[0:pn, :].rearrange("p (g u h) -> p g u h", u=2, h=h)
            em("V", lambda e, src=src, dst=dst, va=va: e.tensor_tensor(
                va(dst)[:, :, 0, :], va(src)[:, :, 0, :], va(src)[:, :, 1, :], ALU.add),
                waits=[t_st, prev_bf], inc="V")
            t_st = em("V", lambda e, src=src, dst=dst, va=va: e.tensor_tensor(
                va(dst)[:, :, 1, :], va(src)[:, :, 0, :], va(src)[:, :, 1, :], ALU.subtract),
                inc="V")
        fin = bufs[nst % 2]
        t_bf = em("V", lambda e, fin=fin, pn=pn: e.tensor_scalar(
            wbf[0:pn, :], fin[0:pn, :], scale, None, ALU.mult), waits=[t_st], inc="V")
        prev_bf = t_bf
        for hk in range(HID // 128):
            t_t = em("T", lambda e, hk=hk, pn=pn: e.transpose(
                pw0b[:, 0:pn], wbf[0:pn, hk * 128:(hk + 1) * 128], idwb[0:pn, 0:pn]),
                waits=[t_bf, t_id, prev_cp], inc="T")
            t_cp = em("V", lambda e, pn=pn: e.tensor_copy(tout[:, 0:pn], pw0b[:, 0:pn]),
                      waits=[t_t, prev_pv], inc="V")
            prev_pv = em("Y", lambda e, hk=hk, p0=p0, pn=pn: e.dma_start(
                out=io["W2hs"][hk * 128:(hk + 1) * 128, p0:p0 + pn], in_=tout[:, 0:pn]),
                waits=[t_cp], inc="dWS", n=16)
            prev_cp = t_cp
    return [k.now("dWS")]



def emit_phaseC(k, cfg, io):
    nc = k.nc
    em = k.emit
    D, HID, DIN, Dout = cfg.D, cfg.HID, cfg.DIN, cfg.Dout
    SPAN = cfg.SPAN
    DINK, DK, HK = DIN // 128, D // 128, HID // 128
    NT = SPAN // 128
    TPP = min(4, NT)
    NPASS = (NT + TPP - 1) // TPP
    RROWS = Dout // cfg.ncores
    g0 = 0
    DOC = Dout // 512 if Dout >= 512 else 1
    DCW = min(512, Dout)

    x1sp = nc.dram_tensor("x1sp", [SPAN, D], F32, kind="Internal").ap()
    xn2T = nc.dram_tensor("xn2T", [D, SPAN], BF16, kind="Internal").ap()

    for s in ["dz0", "dz1", "da0", "da1", "ds0", "ds1", "dn0", "dn1", "dwo",
              "dw10", "dw11", "dw30", "dw31", "du0", "du1", "do0", "do1",
              "dxp0", "dxp1", "dxr0", "dxr1"]:
        k.new_sem(s)

    st = io["stack"]
    sb = lambda name, shape, dt: st.enter_context(nc.sbuf_tensor(name, shape, dt))
    psb = lambda name: st.enter_context(nc.psum_tensor(name, [128, 512], F32))
    pbanks = [psb("pc%d" % i) for i in range(8)]
    csc2 = sb("csc2", [128, 1], F32)
    c_sc2 = nc.inline_tensor(np.full((128, 1), 1e-5, np.float32), "c_sc2")
    idc = sb("idc", [128, 128], BF16)
    c_idc = nc.inline_tensor(np.eye(128, dtype=ml_dtypes.bfloat16), "c_idc2")
    em("Y", lambda e: e.dma_start(out=csc2[:, :], in_=c_sc2.ap()), inc="dC", n=16)
    em("Y", lambda e: e.dma_start(out=idc[:, :], in_=c_idc.ap()), inc="dC", n=16)
    t_cl = k.now("dC")

    def c1():
        st1 = contextlib.ExitStack()
        sb1 = lambda name, shape, dt: st1.enter_context(nc.sbuf_tensor(name, shape, dt))
        wouts = sb1("wouts", [128, DINK * Dout], BF16)
        yzs = [sb1("yzs%d" % i, [128, DINK * 128], BF16) for i in range(2)]
        xtl = [sb1("xtl%d" % i, [128, D], F32) for i in range(2)]
        o1f = sb1("o1f", [128, D], F32)
        sqc = sb1("sqc", [128, D], BF16)
        x1f = sb1("x1f", [128, D], F32)
        xnb = sb1("xnb", [128, D], BF16)
        xnt = sb1("xnt", [128, D], BF16)
        mcol = sb1("mcol", [128, 1], F32)
        qcol = sb1("qcol", [128, 1], F32)
        vcol = sb1("vcol", [128, 1], F32)
        rcol2 = sb1("rcol2", [128, 1], F32)

        wosrc = io.get("Woutbf", io["WoutT"])
        for part in range(8):
            em("P", lambda e, part=part, wosrc=wosrc: e.dma_start(
                out=wouts[:, :].rearrange("p (kk c) -> p kk c", c=Dout)[:, part * (DINK // 8):(part + 1) * (DINK // 8)],
                in_=wosrc.rearrange("(kk p) c -> p kk c", p=128)[:, part * (DINK // 8):(part + 1) * (DINK // 8)]),
                waits=[io.get("t_cast")] if part == 0 else [], inc="dwo", n=16)
        t_wo = k.now("dwo")
        ldz, ldx = {}, {}

        if io.get("yz_cm") is not None:
            yzv = io["yzsp"].rearrange("(s kk p) t -> p (s kk) t", p=128,
                                       s=io["yz_cm"])
            yzv = yzv[:, bass.ts(io["pid"], DINK), :]
        else:
            yzv = io["yzsp"].rearrange("(kk p) t -> p kk t", p=128)
            if io.get("pid") is not None:
                yzv = yzv[:, :, bass.ts(io["pid"], SPAN)]

        def ld_tile(tt):
            par = tt % 2
            ldz[tt] = em("Y", lambda e, tt=tt, par=par: e.dma_start(
                out=yzs[par][:, :].rearrange("p (kk t) -> p kk t", t=128),
                in_=yzv[:, :, tt * 128:(tt + 1) * 128]),
                inc="dz%d" % par, n=16)
            ldx[tt] = em("Y", lambda e, tt=tt, par=par: e.dma_start(
                out=xtl[par][:, :], in_=io["xsp"][tt * 128:(tt + 1) * 128, :]),
                inc="da%d" % par, n=16)

        ld_tile(0)
        s_store = {}
        n_store = {}
        prev_sq = {}
        prev_ev = {}
        prev_trans = {}
        for tt in range(NT):
            par = tt % 2
            if tt + 1 < NT:
                ld_tile(tt + 1)
            for dc in range(DOC):
                for kk in range(DINK):
                    em("T", lambda e, dc=dc, kk=kk, par=par: e.matmul(
                        pbanks[dc][:, 0:DCW], yzs[par][:, kk * 128:(kk + 1) * 128],
                        wouts[:, kk * Dout + dc * DCW:kk * Dout + (dc + 1) * DCW],
                        start=(kk == 0), stop=(kk == DINK - 1)),
                        waits=([ldz[tt], t_wo] + prev_ev.get(tt - 1, [])) if (kk == 0 and dc == 0) else [],
                        inc="T" if kk == DINK - 1 else None)
            t_mm = k.now("T")
            ev = []
            for dc in range(DOC):
                ev.append(em("V", lambda e, dc=dc: e.tensor_copy(
                    o1f[:, dc * DCW:(dc + 1) * DCW], pbanks[dc][:, 0:DCW]),
                    waits=[t_mm] + prev_sq.get(tt - 1, []), inc="V"))
            prev_ev[tt] = ev
            t_s1 = em("V", lambda e: e.tensor_reduce(mcol[:, :], o1f[:, :], AX.X, ALU.add),
                      waits=ev, inc="V")
            t_s2 = em("S", lambda e: e.activation(sqc[:, :], o1f[:, :], AF.Square,
                                                  accum_out=qcol[:, :]),
                      waits=ev + prev_sq.get(tt - 1, []), inc="S")
            t_s3 = em("V", lambda e: e.tensor_scalar(mcol[:, :], mcol[:, :], 1.0 / D, None,
                                                     ALU.mult), waits=[t_s1], inc="V")
            t_s4 = em("V", lambda e: e.tensor_tensor(vcol[:, :], mcol[:, :], mcol[:, :],
                                                     ALU.mult), waits=[t_s3], inc="V")
            t_s5 = em("V", lambda e: e.tensor_scalar(qcol[:, :], qcol[:, :], 1.0 / D, None,
                                                     ALU.mult), waits=[t_s2], inc="V")
            t_s6 = em("V", lambda e: e.tensor_tensor(vcol[:, :], qcol[:, :], vcol[:, :],
                                                     ALU.subtract), waits=[t_s4, t_s5], inc="V")
            t_s7 = em("S", lambda e: e.activation(vcol[:, :], vcol[:, :], AF.Ln,
                                                  bias=csc2[:, :]), waits=[t_s6, t_cl], inc="S")
            t_s8 = em("S", lambda e: e.activation(rcol2[:, :], vcol[:, :], AF.Exp, scale=-0.5),
                      waits=[t_s7], inc="S")
            t_n1 = em("V", lambda e: e.tensor_scalar(o1f[:, :], o1f[:, :], mcol[:, :],
                                                     rcol2[:, :], ALU.subtract, ALU.mult),
                      waits=[t_s8], inc="V")
            t_x1 = em("V", lambda e, par=par: e.tensor_tensor(x1f[:, :], o1f[:, :],
                                                              xtl[par][:, :], ALU.add),
                      waits=[t_n1, ldx[tt], s_store.get(tt - 1)], inc="V")
            s_store[tt] = em("Y", lambda e, tt=tt: e.dma_start(
                out=x1sp[tt * 128:(tt + 1) * 128, :], in_=x1f[:, :]),
                waits=[t_x1], inc="ds%d" % par, n=16)
            t_m1 = em("V", lambda e: e.tensor_reduce(mcol[:, :], x1f[:, :], AX.X, ALU.add),
                      waits=[t_x1], inc="V")
            t_m2 = em("S", lambda e: e.activation(sqc[:, :], x1f[:, :], AF.Square,
                                                  accum_out=qcol[:, :]),
                      waits=[t_x1, t_s2, t_s5], inc="S")
            t_m3 = em("V", lambda e: e.tensor_scalar(mcol[:, :], mcol[:, :], 1.0 / D, None,
                                                     ALU.mult), waits=[t_m1], inc="V")
            t_m4 = em("V", lambda e: e.tensor_tensor(vcol[:, :], mcol[:, :], mcol[:, :],
                                                     ALU.mult), waits=[t_m3], inc="V")
            t_m5 = em("V", lambda e: e.tensor_scalar(qcol[:, :], qcol[:, :], 1.0 / D, None,
                                                     ALU.mult), waits=[t_m2], inc="V")
            t_m6 = em("V", lambda e: e.tensor_tensor(vcol[:, :], qcol[:, :], vcol[:, :],
                                                     ALU.subtract), waits=[t_m4, t_m5], inc="V")
            t_m7 = em("S", lambda e: e.activation(vcol[:, :], vcol[:, :], AF.Ln,
                                                  bias=csc2[:, :]), waits=[t_m6], inc="S")
            t_m8 = em("S", lambda e: e.activation(rcol2[:, :], vcol[:, :], AF.Exp, scale=-0.5),
                      waits=[t_m7], inc="S")
            t_n2 = em("V", lambda e: e.tensor_scalar(xnb[:, :], x1f[:, :], mcol[:, :],
                                                     rcol2[:, :], ALU.subtract, ALU.mult),
                      waits=[t_m8], inc="V")
            cp_tags = {}
            for kk in range(DK):
                bank = pbanks[4 + (kk % 2)]
                em("T", lambda e, kk=kk, bank=bank: e.transpose(
                    bank.ap().bitcast(BF16)[:, 0:128],
                    xnb[:, kk * 128:(kk + 1) * 128], idc[:, :]),
                    waits=[t_n2, cp_tags.get(kk - 2)] + (prev_trans.get(tt - 1, []) if kk < 2 else []),
                    inc="T")
                tb = k.now("T")
                cp_tags[kk] = em("V", lambda e, kk=kk, bank=bank: e.tensor_copy(
                    xnt[:, kk * 128:(kk + 1) * 128], bank.ap().bitcast(BF16)[:, 0:128]),
                    waits=[tb, n_store.get(tt - 1)], inc="V")
            t_tn = k.now("V")
            n_store[tt] = em("Y", lambda e, tt=tt: e.dma_start(
                out=xn2T.rearrange("(kk p) t -> p kk t", p=128)[:, :, tt * 128:(tt + 1) * 128],
                in_=xnt[:, :].rearrange("p (kk t) -> p kk t", t=128)),
                waits=[t_tn, n_store.get(tt - 2)], inc="dn%d" % par, n=16)
            prev_sq[tt] = [t_m2, t_m5, t_n2]
            prev_trans[tt] = [cp_tags[DK - 2], cp_tags[DK - 1]]
        end_prev = [k.now(s) for s in ("T", "V", "S", "P")]
        st1.close()
        return end_prev

    end_c1 = c1()

    t_allx1 = [k.now("ds0"), k.now("ds1")]
    t_alln = [k.now("dn0"), k.now("dn1")]

    st2 = contextlib.ExitStack()
    sb2 = lambda name, shape, dt: st2.enter_context(nc.sbuf_tensor(name, shape, dt))
    HCB = 1
    xnp = [sb2("xnp%d" % i, [128, DK * TPP * 128], BF16) for i in range(2)]
    hidb = sb2("hidb", [128, HK * TPP * 128], BF16)
    w1c = [sb2("w1c%d" % i, [128, DK * 128], BF16) for i in range(2)]
    w3c = [sb2("w3c%d" % i, [128, DK * 128], BF16) for i in range(2)]
    w2c = [sb2("w2c%d" % i, [128, Dout // 2], BF16) for i in range(2)]
    o2acc = sb2("o2acc", [128, TPP * Dout], F32)
    ebf = sb2("ebf", [128, TPP * 128], BF16)
    den = sb2("den", [128, TPP * 128], F32)
    hv = sb2("hv", [128, TPP * 128], F32)
    o2f = sb2("o2f", [128, Dout], F32)
    sq2 = sb2("sq2", [128, Dout], BF16)
    x1r = [sb2("x1r%d" % i, [128, D], F32) for i in range(2)]
    ofin = sb2("ofin", [128, Dout], F32)
    mc2 = sb2("mc2", [128, 1], F32)
    qc2 = sb2("qc2", [128, 1], F32)
    vc2 = sb2("vc2", [128, 1], F32)
    rc2 = sb2("rc2", [128, 1], F32)

    k.fence(end_c1 + t_allx1 + t_alln)
    end_prev = [k.now(s) for s in ("T", "V", "S", "P")]
    o_store = {}
    for pss in range(NPASS):
        ppar = pss % 2
        tiles = list(range(pss * TPP, min((pss + 1) * TPP, NT)))
        ntl = len(tiles)
        TWID = ntl * 128
        k.fence(end_prev)
        t_xp = em("Y", lambda e, t0=tiles[0] * 128, ntl=ntl, ppar=ppar: e.dma_start(
            out=xnp[ppar][:, 0:DK * ntl * 128].rearrange("p (kk t) -> p kk t", t=ntl * 128),
            in_=xn2T.rearrange("(kk p) t -> p kk t", p=128)[:, :, t0:t0 + ntl * 128]),
            inc="dxp%d" % ppar, n=16)
        w1t, w3t, gtag, ltag = {}, {}, {}, {}

        w1src = io.get("W1bf", io["W1T"])
        w3src = io.get("W3bf", io["W3T"])

        def ld_w(hc):
            wpar = hc % 2
            w1t[hc] = em("P", lambda e, hc=hc, wpar=wpar: e.dma_start(
                out=w1c[wpar][:, :].rearrange("p (kk h) -> p kk h", h=128),
                in_=w1src.rearrange("(kk p) h -> p kk h", p=128)[:, :, hc * 128:(hc + 1) * 128]),
                waits=[gtag.get(hc - 2), io.get("t_cast")], inc="dw1%d" % wpar, n=16)
            w3t[hc] = em("P", lambda e, hc=hc, wpar=wpar: e.dma_start(
                out=w3c[wpar][:, :].rearrange("p (kk h) -> p kk h", h=128),
                in_=w3src.rearrange("(kk p) h -> p kk h", p=128)[:, :, hc * 128:(hc + 1) * 128]),
                waits=[ltag.get(hc - 2), io.get("t_cast")], inc="dw3%d" % wpar, n=16)

        ld_w(0)
        hfree = {}
        prev_d = None
        for hc in range(HK):
            wpar = hc % 2
            if hc + 1 < HK:
                ld_w(hc + 1)
            bg = pbanks[2 * (hc % 2)]
            bl = pbanks[2 * (hc % 2) + 1]
            for kk in range(DK):
                em("T", lambda e, kk=kk, hc=hc, wpar=wpar, ppar=ppar, bg=bg, TWID=TWID: e.matmul(
                    bg[:, 0:TWID], w1c[wpar][:, kk * 128:(kk + 1) * 128],
                    xnp[ppar][:, kk * TWID:(kk + 1) * TWID],
                    start=(kk == 0), stop=(kk == DK - 1)),
                    waits=[w1t[hc], t_xp, hfree.get(hc - 2)] if kk == 0 else [],
                    inc="T" if kk == DK - 1 else None)
            t_g = k.now("T")
            gtag[hc] = t_g
            for kk in range(DK):
                em("T", lambda e, kk=kk, hc=hc, wpar=wpar, ppar=ppar, bl=bl, TWID=TWID: e.matmul(
                    bl[:, 0:TWID], w3c[wpar][:, kk * 128:(kk + 1) * 128],
                    xnp[ppar][:, kk * TWID:(kk + 1) * TWID],
                    start=(kk == 0), stop=(kk == DK - 1)),
                    waits=[w3t[hc]] if kk == 0 else [],
                    inc="T" if kk == DK - 1 else None)
            t_l = k.now("T")
            ltag[hc] = t_l
            t_e = em("S", lambda e, bg=bg, TWID=TWID: e.activation(
                ebf[:, 0:TWID], bg[:, 0:TWID], AF.Tanh, scale=0.5),
                waits=[t_g, prev_d], inc="S")
            t_d = em("V", lambda e, bg=bg, TWID=TWID: e.tensor_scalar(
                den[:, 0:TWID], bg[:, 0:TWID], 0.5, None, ALU.mult),
                waits=[t_g, t_e], inc="V")
            t_h1 = em("V", lambda e, TWID=TWID: e.scalar_tensor_tensor(
                hv[:, 0:TWID], ebf[:, 0:TWID], 1.0, den[:, 0:TWID], ALU.add, ALU.mult),
                waits=[t_e, t_d], inc="V")
            t_h2 = em("V", lambda e, bl=bl, hc=hc, TWID=TWID: e.tensor_tensor(
                hidb[:, hc * TWID:(hc + 1) * TWID], hv[:, 0:TWID], bl[:, 0:TWID], ALU.mult),
                waits=[t_h1, t_l], inc="V")
            prev_d = t_h1
            hfree[hc] = t_h2
        t_hid = k.now("V")

        HWID = Dout // 2
        DCW2 = min(512, HWID)
        NCD = HWID // DCW2
        w2l = {}

        def ld_w2(key, hk, half, extra=None):
            wpar = hk % 2
            w2l[key] = em("Y", lambda e, hk=hk, half=half, wpar=wpar: e.dma_start(
                out=w2c[wpar][:, :].rearrange("p (r rr) -> p r rr", r=cfg.ncores // 2),
                in_=io["W2hG"].rearrange("(r h) rr -> h r rr", r=cfg.ncores)
                    [hk * 128:(hk + 1) * 128, half * (cfg.ncores // 2):(half + 1) * (cfg.ncores // 2)]),
                waits=[extra], inc="du%d" % wpar, n=16)

        t_lastmm = None
        ev_half = {}
        for half in range(2):
            mm_hk = {}
            ld_w2((half, 0), 0, half, t_lastmm)
            for hk in range(HK):
                if hk + 1 < HK:
                    ld_w2((half, hk + 1), hk + 1, half, mm_hk.get(hk - 1, t_lastmm))
                for ti in range(ntl):
                    for dc in range(NCD):
                        bank = pbanks[ti * NCD + dc]
                        em("T", lambda e, hk=hk, ti=ti, dc=dc, bank=bank, TWID=TWID: e.matmul(
                            bank[:, 0:DCW2],
                            hidb[:, hk * TWID + ti * 128:hk * TWID + ti * 128 + 128],
                            w2c[hk % 2][:, dc * DCW2:(dc + 1) * DCW2],
                            start=(hk == 0), stop=(hk == HK - 1)),
                            waits=([w2l[(half, hk)], t_hid] +
                                   ([ev_half[0]] if (half == 1 and hk == 0) else []))
                                  if (ti == 0 and dc == 0) else [],
                            inc="T" if (ti == ntl - 1 and dc == NCD - 1) else None)
                mm_hk[hk] = k.now("T")
            t_lastmm = mm_hk[HK - 1]
            ev2 = []
            for ti in range(ntl):
                for dc in range(NCD):
                    ev2.append(em("V", lambda e, ti=ti, dc=dc, half=half: e.tensor_copy(
                        o2acc[:, ti * Dout + half * HWID + dc * DCW2:
                              ti * Dout + half * HWID + (dc + 1) * DCW2],
                        pbanks[ti * NCD + dc][:, 0:DCW2]),
                        waits=[t_lastmm] if (ti == 0 and dc == 0) else [], inc="V"))
            ev_half[half] = ev2[-1]
        for ti in range(ntl):
            tt = tiles[ti]
            par2 = tt % 2
            ov = o2acc[:, ti * Dout:(ti + 1) * Dout]
            t_lx = em("Y", lambda e, tt=tt, par2=par2: e.dma_start(
                out=x1r[par2][:, :], in_=x1sp[tt * 128:(tt + 1) * 128, :]),
                waits=[o_store.get(tt - 2)], inc="dxr%d" % par2, n=16)
            t_a1 = em("V", lambda e, ov=ov: e.tensor_reduce(mc2[:, :], ov, AX.X, ALU.add),
                      waits=[ev_half[0], ev_half[1]], inc="V")
            t_a2 = em("S", lambda e, ov=ov: e.activation(sq2[:, :], ov, AF.Square,
                                                  accum_out=qc2[:, :]),
                      waits=[ev_half[0], ev_half[1]], inc="S")
            t_a3 = em("V", lambda e: e.tensor_scalar(mc2[:, :], mc2[:, :], 1.0 / Dout, None,
                                                     ALU.mult), waits=[t_a1], inc="V")
            t_a4 = em("V", lambda e: e.tensor_tensor(vc2[:, :], mc2[:, :], mc2[:, :],
                                                     ALU.mult), waits=[t_a3], inc="V")
            t_a5 = em("V", lambda e: e.tensor_scalar(qc2[:, :], qc2[:, :], 1.0 / Dout, None,
                                                     ALU.mult), waits=[t_a2], inc="V")
            t_a6 = em("V", lambda e: e.tensor_tensor(vc2[:, :], qc2[:, :], vc2[:, :],
                                                     ALU.subtract), waits=[t_a4, t_a5], inc="V")
            t_a7 = em("S", lambda e: e.activation(vc2[:, :], vc2[:, :], AF.Ln,
                                                  bias=csc2[:, :]), waits=[t_a6], inc="S")
            t_a8 = em("S", lambda e: e.activation(rc2[:, :], vc2[:, :], AF.Exp, scale=-0.5),
                      waits=[t_a7], inc="S")
            t_a9 = em("V", lambda e, ov=ov: e.tensor_scalar(ov, ov, mc2[:, :],
                                                     rc2[:, :], ALU.subtract, ALU.mult),
                      waits=[t_a8], inc="V")
            t_fo = em("V", lambda e, par2=par2, ov=ov: e.tensor_tensor(
                ofin[:, :], ov, x1r[par2][:, :], ALU.add),
                waits=[t_a9, t_lx, o_store.get(tt - 1)], inc="V")
            o_store[tt] = em("Y", lambda e, tt=tt, par2=par2: e.dma_start(
                out=io["outsp"][tt * 128:(tt + 1) * 128, :], in_=ofin[:, :]),
                waits=[t_fo], inc="do%d" % par2, n=16)
            k.fence([t_fo], engines=("V", "S"))
        end_prev = [k.now(s) for s in ("T", "V", "S", "P")]
    st2.close()
    return [k.now("do0"), k.now("do1")]


def build_launch1(cfg):
    nc = bass.Bass("TRN2", target_bir_lowering=False)
    k = K(nc)
    RROWS = cfg.Dout // cfg.ncores
    io = {}
    io["xT"] = nc.dram_tensor("xT", [cfg.D, cfg.Ltot], F32, kind="ExternalInput").ap()
    io["WcsT"] = nc.dram_tensor("WcsT", [cfg.D, cfg.WROWS], F32, kind="ExternalInput").ap()
    for nm, shape in [("Alog", [1, cfg.HPC]), ("Bnw", [1, cfg.N]), ("Cnw", [1, cfg.N]),
                      ("Bb", [1, cfg.HPC * cfg.N]), ("Cb", [1, cfg.HPC * cfg.N])]:
        io[nm] = nc.dram_tensor(nm, shape, F32, kind="ExternalInput").ap()
    io["W2s"] = nc.dram_tensor("W2s", [RROWS, cfg.HID], F32, kind="ExternalInput").ap()
    io["yzS"] = nc.dram_tensor("yzS", [cfg.HPC * cfg.P, cfg.Ltot], BF16,
                               kind="ExternalOutput").ap()
    io["W2hs"] = nc.dram_tensor("W2hs", [cfg.HID, RROWS], BF16, kind="ExternalOutput").ap()
    with contextlib.ExitStack() as stw:
        iow = dict(io)
        iow["stack"] = stw
        tw = emit_phaseW(k, cfg, iow)
    k.fence([k.now(s) for s in ("T", "V", "S", "P")] + tw)
    with contextlib.ExitStack() as sta:
        ioa = dict(io)
        ioa["stack"] = sta
        ta = emit_phaseA(k, cfg, ioa)
        k.emit("Y", None, waits=ta + tw)
        k.replay()
    return nc


def build_launch2(cfg):
    nc = bass.Bass("TRN2", target_bir_lowering=False)
    k = K(nc)
    RROWS = cfg.Dout // cfg.ncores
    io = {}
    io["yzsp"] = nc.dram_tensor("yzsp", [cfg.DIN, cfg.SPAN], BF16, kind="ExternalInput").ap()
    io["WoutT"] = nc.dram_tensor("WoutT", [cfg.DIN, cfg.Dout], F32, kind="ExternalInput").ap()
    io["W1T"] = nc.dram_tensor("W1T", [cfg.D, cfg.HID], F32, kind="ExternalInput").ap()
    io["W3T"] = nc.dram_tensor("W3T", [cfg.D, cfg.HID], F32, kind="ExternalInput").ap()
    io["W2hG"] = nc.dram_tensor("W2hG", [cfg.ncores * cfg.HID, RROWS], BF16,
                                kind="ExternalInput").ap()
    io["xsp"] = nc.dram_tensor("xsp", [cfg.SPAN, cfg.D], F32, kind="ExternalInput").ap()
    io["outsp"] = nc.dram_tensor("outsp", [cfg.SPAN, cfg.Dout], F32, kind="ExternalOutput").ap()
    with contextlib.ExitStack() as stack:
        io["stack"] = stack
        tc = emit_phaseC(k, cfg, io)
        k.emit("Y", None, waits=tc)
        k.replay()
    return nc



def _deinterleave_rows(w, N):
    H = w.shape[0] // N
    w = w.reshape(H, N, -1)
    perm = np.concatenate([np.arange(0, N, 2), np.arange(1, N, 2)])
    return w[:, perm].reshape(H * N, -1)


def pack_core_inputs(cfg, core, ins):
    hp = cfg.HPC
    h0 = core * hp
    P, N = cfg.P, cfg.N
    f = lambda nm: np.asarray(ins[nm], np.float32)
    Wx = f("L0_attn_Wx")[h0 * P:(h0 + hp) * P]
    Wz = f("L0_attn_Wz")[h0 * P:(h0 + hp) * P]
    WB = _deinterleave_rows(f("L0_attn_WB")[h0 * N:(h0 + hp) * N], N)
    WC = _deinterleave_rows(f("L0_attn_WC")[h0 * N:(h0 + hp) * N], N)
    Wth = f("L0_attn_Wth")[h0 * (N // 2):(h0 + hp) * (N // 2)]
    Wdt = f("L0_attn_Wdt")[h0:h0 + hp]
    Wlam = f("L0_attn_Wlam")[h0:h0 + hp]
    Wcat = np.concatenate([Wx, Wz, WB, WC, Wth, Wdt, Wlam], axis=0)
    assert Wcat.shape == (cfg.WROWS, cfg.D)
    perm = np.concatenate([np.arange(0, N, 2), np.arange(1, N, 2)])
    return {
        "WcsT": np.ascontiguousarray(Wcat.T),
        "Alog": np.ascontiguousarray(f("L0_attn_A_log")[h0:h0 + hp].reshape(1, hp)),
        "Bnw": np.ascontiguousarray(f("L0_attn_Bn_w")[perm].reshape(1, N)),
        "Cnw": np.ascontiguousarray(f("L0_attn_Cn_w")[perm].reshape(1, N)),
        "Bb": np.ascontiguousarray(f("L0_attn_B_bias")[h0:h0 + hp][:, perm].reshape(1, hp * N)),
        "Cb": np.ascontiguousarray(f("L0_attn_C_bias")[h0:h0 + hp][:, perm].reshape(1, hp * N)),
    }


def host_pack_A(cfg, ins):
    x = np.asarray(ins["L0_attn_x"], np.float32).reshape(cfg.Ltot, cfg.D)
    xT = np.ascontiguousarray(x.T)
    RROWS = cfg.Dout // cfg.ncores
    W2 = np.asarray(ins["L0_ffn_W2"], np.float32)
    maps = []
    for c in range(cfg.ncores):
        d = pack_core_inputs(cfg, c, ins)
        m = {
            "xT": xT,
            "WcsT": d["WcsT"],
            "Alog": d["Alog"], "Bnw": d["Bnw"], "Cnw": d["Cnw"],
            "Bb": d["Bb"], "Cb": d["Cb"],
            "W2s": np.ascontiguousarray(W2[c * RROWS:(c + 1) * RROWS]),
        }
        maps.append(m)
    return maps, x


def host_pack_C(cfg, ins, x, yzS_list, W2hs_list):
    WoutT = np.ascontiguousarray(np.asarray(ins["L0_attn_Wout"], np.float32).T)
    W1T = np.ascontiguousarray(np.asarray(ins["L0_ffn_W1"], np.float32).T)
    W3T = np.ascontiguousarray(np.asarray(ins["L0_ffn_W3"], np.float32).T)
    W2hG = np.ascontiguousarray(np.concatenate(W2hs_list, axis=0))
    XW = cfg.HPC * cfg.P
    maps = []
    for c in range(cfg.ncores):
        yzsp = np.ascontiguousarray(
            np.concatenate([yzS_list[r][c * XW:(c + 1) * XW] for r in range(cfg.ncores)],
                           axis=0))
        maps.append({
            "yzsp": yzsp,
            "WoutT": WoutT, "W1T": W1T, "W3T": W3T, "W2hG": W2hG,
            "xsp": np.ascontiguousarray(x[c * cfg.SPAN:(c + 1) * cfg.SPAN]),
        })
    return maps


def kernel_unfused(inputs):
    cfg = Cfg()
    maps1, x = host_pack_A(cfg, inputs)
    nc1 = build_launch1(cfg)
    res1 = run_bass_kernel_spmd(nc1, maps1, core_ids=list(range(cfg.ncores)))
    yzT_list = [r["yzS"] for r in res1.results]
    W2hs_list = [r["W2hs"] for r in res1.results]
    maps2 = host_pack_C_unfused(cfg, inputs, x, yzT_list, W2hs_list)
    nc2 = build_launch2(cfg)
    res2 = run_bass_kernel_spmd(nc2, maps2, core_ids=list(range(cfg.ncores)))
    out = np.concatenate([np.asarray(r["outsp"], np.float32) for r in res2.results], axis=0)
    return out.reshape(cfg.B, cfg.L, cfg.Dout)


def kernel(**inputs):
    return kernel_fused(inputs)


def host_pack_C_unfused(cfg, ins, x, yzT_list, W2hs_list):
    WoutT = np.ascontiguousarray(np.asarray(ins["L0_attn_Wout"], np.float32).T)
    W1T = np.ascontiguousarray(np.asarray(ins["L0_ffn_W1"], np.float32).T)
    W3T = np.ascontiguousarray(np.asarray(ins["L0_ffn_W3"], np.float32).T)
    W2hG = np.ascontiguousarray(np.concatenate(W2hs_list, axis=0))
    maps = []
    for c in range(cfg.ncores):
        yzsp = np.ascontiguousarray(
            np.concatenate([yzT_list[r][:, c * cfg.SPAN:(c + 1) * cfg.SPAN]
                            for r in range(cfg.ncores)], axis=0))
        maps.append({
            "yzsp": yzsp,
            "WoutT": WoutT, "W1T": W1T, "W3T": W3T, "W2hG": W2hG,
            "xsp": np.ascontiguousarray(x[c * cfg.SPAN:(c + 1) * cfg.SPAN]),
        })
    return maps


def build_fused(cfg):
    nc = bass.Bass("TRN2", target_bir_lowering=False, num_devices=cfg.ncores)
    k = K(nc)
    RROWS = cfg.Dout // cfg.ncores
    XW = cfg.HPC * cfg.P
    io = {}
    io["xT"] = nc.dram_tensor("xT", [cfg.D, cfg.Ltot], F32, kind="ExternalInput").ap()
    io["WcsT"] = nc.dram_tensor("WcsT", [cfg.D, cfg.WROWS], F32, kind="ExternalInput").ap()
    for nm, shape in [("Alog", [1, cfg.HPC]), ("Bnw", [1, cfg.N]), ("Cnw", [1, cfg.N]),
                      ("Bb", [1, cfg.HPC * cfg.N]), ("Cb", [1, cfg.HPC * cfg.N])]:
        io[nm] = nc.dram_tensor(nm, shape, F32, kind="ExternalInput").ap()
    io["W2s"] = nc.dram_tensor("W2s", [RROWS, cfg.HID], F32, kind="ExternalInput").ap()
    io["WoutT"] = nc.dram_tensor("WoutT", [cfg.DIN, cfg.Dout], F32, kind="ExternalInput").ap()
    io["W1T"] = nc.dram_tensor("W1T", [cfg.D, cfg.HID], F32, kind="ExternalInput").ap()
    io["W3T"] = nc.dram_tensor("W3T", [cfg.D, cfg.HID], F32, kind="ExternalInput").ap()
    io["xsp"] = nc.dram_tensor("xsp", [cfg.SPAN, cfg.D], F32, kind="ExternalInput").ap()
    io["outsp"] = nc.dram_tensor("outsp", [cfg.SPAN, cfg.Dout], F32,
                                 kind="ExternalOutput").ap()
    NSCC = 8
    LCH = cfg.Ltot // NSCC
    yzT_loc = nc.dram_tensor("yzT_loc", [NSCC * XW, LCH], BF16, kind="Internal").ap()
    yzA = nc.dram_tensor("yzA", [NSCC * cfg.ncores * XW, LCH], BF16, kind="Internal",
                         addr_space="Shared").ap()
    w2hs_loc = nc.dram_tensor("w2hs_loc", [cfg.HID, RROWS], BF16, kind="Internal").ap()
    w2hG = nc.dram_tensor("w2hG", [cfg.ncores * cfg.HID, RROWS], BF16, kind="Internal",
                          addr_space="Shared").ap()
    io["yzS_cm"] = yzT_loc
    io["LCH"] = LCH
    io["yz_cm"] = NSCC
    io["W2hs"] = w2hs_loc
    io["yzsp"] = yzA
    io["W2hG"] = w2hG

    with contextlib.ExitStack() as stw:
        iow = dict(io)
        iow["stack"] = stw
        tw = emit_phaseW(k, cfg, iow)
    rg = [list(range(cfg.ncores))]
    k.emit("P", lambda e: e.collective_compute(
        "AllGather", ALU.bypass, replica_groups=rg,
        ins=[w2hs_loc], outs=[w2hG]), waits=tw, inc="cc", n=1)
    t_cc1 = k.now("cc")
    w1bf = nc.dram_tensor("w1bf", [cfg.D, cfg.HID], BF16, kind="Internal").ap()
    w3bf = nc.dram_tensor("w3bf", [cfg.D, cfg.HID], BF16, kind="Internal").ap()
    wobf = nc.dram_tensor("wobf", [cfg.DIN, cfg.Dout], BF16, kind="Internal").ap()
    k.new_sem("dct")
    for r0 in range(0, cfg.D, 128):
        k.emit("P", lambda e, r0=r0: e.dma_start(
            out=w1bf[r0:r0 + 128, :], in_=io["W1T"][r0:r0 + 128, :]), inc="dct", n=16)
        k.emit("P", lambda e, r0=r0: e.dma_start(
            out=w3bf[r0:r0 + 128, :], in_=io["W3T"][r0:r0 + 128, :]), inc="dct", n=16)
    for r0 in range(0, cfg.DIN, 128):
        k.emit("P", lambda e, r0=r0: e.dma_start(
            out=wobf[r0:r0 + 128, :], in_=io["WoutT"][r0:r0 + 128, :]), inc="dct", n=16)
    io["t_cast"] = k.now("dct")
    io["W1bf"], io["W3bf"], io["Woutbf"] = w1bf, w3bf, wobf
    k.fence([k.now(s) for s in ("T", "V", "S", "P")] + tw)
    mil = (cfg.B * cfg.NQ) // NSCC
    GW = cfg.ncores * XW
    assert LCH == cfg.SPAN

    def yz_cb(j, store_tags):
        if (j + 1) % mil:
            return
        ci = (j + 1) // mil - 1
        w = [store_tags[j]]
        if j - 1 in store_tags:
            w.append(store_tags[j - 1])
        k.emit("P", lambda e, ci=ci: e.collective_compute(
            "AllGather", ALU.bypass, replica_groups=rg,
            ins=[yzT_loc[ci * XW:(ci + 1) * XW, :]],
            outs=[yzA[ci * GW:(ci + 1) * GW, :]]), waits=w, inc="cc", n=1)
    with contextlib.ExitStack() as sta:
        ioa = dict(io)
        ioa["stack"] = sta
        ioa["yz_cb"] = yz_cb
        ta = emit_phaseA(k, cfg, ioa)
    t_cc2 = k.now("cc")
    k.fence([k.now(s) for s in ("T", "V", "S", "P")] + [t_cc1, t_cc2])
    io["pid"] = nc.partition_id()
    with contextlib.ExitStack() as stc:
        ioc = dict(io)
        ioc["stack"] = stc
        tc = emit_phaseC(k, cfg, ioc)
        k.emit("Y", None, waits=tc)
        k.replay()
    return nc


def kernel_fused(inputs):
    cfg = Cfg()
    maps1, x = host_pack_A(cfg, inputs)
    WoutT = np.ascontiguousarray(np.asarray(inputs["L0_attn_Wout"], np.float32).T)
    W1T = np.ascontiguousarray(np.asarray(inputs["L0_ffn_W1"], np.float32).T)
    W3T = np.ascontiguousarray(np.asarray(inputs["L0_ffn_W3"], np.float32).T)
    for c in range(cfg.ncores):
        maps1[c].update({
            "WoutT": WoutT, "W1T": W1T, "W3T": W3T,
            "xsp": np.ascontiguousarray(x[c * cfg.SPAN:(c + 1) * cfg.SPAN]),
        })
    nc = build_fused(cfg)
    res = run_bass_kernel_spmd(nc, maps1, core_ids=list(range(cfg.ncores)))
    out = np.concatenate([np.asarray(r["outsp"], np.float32) for r in res.results], axis=0)
    return out.reshape(cfg.B, cfg.L, cfg.Dout)


# version 17
# speedup vs baseline: 1.0910x; 1.0910x over previous
import contextlib
import math
import numpy as np
import ml_dtypes
import concourse.bass as bass
import concourse.mybir as mybir
from concourse.bass_utils import run_bass_kernel_spmd

F32 = mybir.dt.float32
BF16 = mybir.dt.bfloat16
AF = mybir.ActivationFunctionType
ALU = mybir.AluOpType
AX = mybir.AxisListType

MAGIC = float(np.float32(1.5 * 2 ** 23))
TWO_PI = float(np.float32(2 * np.pi))
INV_2PI = float(np.float32(1.0 / (2 * np.pi)))
HALF_PI = float(np.float32(np.pi / 2))
NCORES = 8


class Cfg:
    def __init__(self, B=2, L=8192, D=2048, H=64, N=128, P=64, HID=8192,
                 ncores=NCORES, Dout=None):
        self.B, self.L, self.D, self.H, self.N, self.P, self.HID = B, L, D, H, N, P, HID
        self.DIN = H * P
        self.Dout = D if Dout is None else Dout
        self.ncores = ncores
        self.HPC = H // ncores
        self.Q = 128
        assert L % self.Q == 0 and D % 128 == 0 and N == 128 and P == 64
        self.NQ = L // self.Q
        self.Ltot = B * L
        self.DK = D // 128
        hp = self.HPC
        self.sec = {}
        off = 0
        for name, w in [("xp", hp * P), ("z", hp * P), ("B", hp * N), ("C", hp * N),
                        ("th", hp * (N // 2)), ("dt", hp), ("lam", hp)]:
            self.sec[name] = (off, w)
            off += w
        self.WROWS = off
        self.chunks = []
        for name in ["xp", "z", "B", "C", "th"]:
            o, w = self.sec[name]
            while w > 0:
                cw = min(512, w)
                self.chunks.append((o, cw))
                o += cw
                w -= cw
        self.chunks.append((self.sec["dt"][0], 2 * hp))
        self.NCH = len(self.chunks)
        self.SPAN = (B * L) // ncores


CFG = Cfg()
ENGS = ("T", "V", "S", "P", "Y")


class K:

    def __init__(self, nc):
        self.nc = nc
        self.ops = {e: [] for e in ENGS}
        self.semnames = []
        self.cnt = {}
        self.hwm = {e: {} for e in ENGS}
        self.sems = {}
        for s in ("T", "V", "S", "P", "dC", "dR", "dW", "dX0", "dX1", "dY0", "dY1", "dA", "cc"):
            self.new_sem(s)

    def new_sem(self, name):
        if name not in self.cnt:
            self.semnames.append(name)
            self.cnt[name] = 0

    def emit(self, eng, fn, waits=(), inc=None, n=1):
        ws = []
        h = self.hwm[eng]
        for wv in waits:
            if wv is None:
                continue
            s, v = wv
            if h.get(s, 0) < v:
                ws.append((s, v))
                h[s] = v
        tag = None
        if inc is not None:
            self.cnt[inc] += n
            tag = (inc, self.cnt[inc])

        def run(e, ws=tuple(ws), fn=fn, inc=inc, n=n):
            for s, v in ws:
                e.wait_ge(self.sems[s], v)
            inst = fn(e) if fn is not None else None
            if inc is not None:
                inst.then_inc(self.sems[inc], n)
        self.ops[eng].append(run)
        return tag

    def now(self, sem):
        return (sem, self.cnt[sem])

    def fence(self, tags, engines=ENGS):
        for e in engines:
            self.emit(e, None, waits=tags)

    def replay(self):
        nc = self.nc
        with contextlib.ExitStack() as st:
            for name in self.semnames:
                self.sems[name] = st.enter_context(nc.semaphore(name))
            with nc.Block() as block:
                def mk(engname):
                    def body(e):
                        for f in self.ops[engname]:
                            f(e)
                    return body
                block.tensor(mk("T"))
                block.vector(mk("V"))
                block.scalar(mk("S"))
                block.gpsimd(mk("P"))
                block.sync(mk("Y"))


def bcast_h(ap2d, hpc, w):
    return ap2d.unsqueeze(2).broadcast_to([ap2d.shape[0], hpc, w])


def bcast_rep(ap2d, hpc, w):
    return ap2d.unsqueeze(1).broadcast_to([ap2d.shape[0], hpc, w])


def v3(ap2d, hpc, w):
    return ap2d.rearrange("p (h w) -> p h w", h=hpc)



def emit_phaseA(k, cfg, io):
    nc = k.nc
    em = k.emit
    D, DK, Q, HPC, N, P = cfg.D, cfg.DK, cfg.Q, cfg.HPC, cfg.N, cfg.P
    WROWS = cfg.WROWS
    XW = HPC * P
    BW = HPC * N
    TW = HPC * (N // 2)
    n_inst = cfg.B * cfg.NQ

    c_ul = nc.inline_tensor(np.triu(np.ones((128, 128), np.float32)), "c_ul")
    c_astr = nc.inline_tensor(np.tril(np.ones((128, 128)), -1).astype(ml_dtypes.bfloat16), "c_astr")
    c_s1 = nc.inline_tensor(np.eye(128, k=-1, dtype=np.float32), "c_s1")
    c_m01 = nc.inline_tensor(
        np.triu(np.full((128, 128), -1e30)).astype(ml_dtypes.bfloat16), "c_m01")
    c_onesf = nc.inline_tensor(np.ones((128, 128), np.float32), "c_onesf")
    c_onesr = nc.inline_tensor(np.ones((1, 128), np.float32), "c_onesr")
    c_e127 = nc.inline_tensor(np.eye(128, dtype=np.float32)[127:128].copy(), "c_e127")
    c_idb = nc.inline_tensor(np.eye(128, dtype=ml_dtypes.bfloat16), "c_idb")
    cvals = np.array([1e-5, 1e-6, 1.0, HALF_PI], np.float32)
    c_sc = nc.inline_tensor(np.tile(cvals[None, :], (128, 1)), "c_sc")
    I_EPSLN, I_EPSRMS, I_ONE, I_HALFPI = 0, 1, 2, 3

    st = io["stack"]
    sb = lambda name, shape, dt: st.enter_context(nc.sbuf_tensor(name, shape, dt))
    psb = lambda name: st.enter_context(nc.psum_tensor(name, [128, 512], F32))

    ul = sb("ul", [128, 128], F32)
    astr = sb("astr", [128, 128], BF16)
    s1m = sb("s1m", [128, 128], F32)
    m01 = sb("m01", [128, 128], BF16)
    onesf = sb("onesf", [128, 128], F32)
    onesr = sb("onesr", [1, 128], F32)
    e127 = sb("e127", [1, 128], F32)
    idb = sb("idb", [128, 128], BF16)
    csc = sb("csc", [128, 4], F32)
    onescol = sb("onescol", [128, 1], F32)
    onescolb = sb("onescolb", [128, 1], BF16)
    wcs = sb("wcs", [128, DK * WROWS], BF16)
    wsum = sb("wsum", [1, WROWS], BF16)
    arep = sb("arep", [128, HPC], F32)
    wbrep = sb("wbrep", [128, N], BF16)
    wcrep = sb("wcrep", [128, N], BF16)
    bbrep = sb("bbrep", [128, HPC * N], BF16)
    cbrep = sb("cbrep", [128, HPC * N], BF16)
    hall = sb("hall", [128, HPC * P], F32)
    hallb = sb("hallb", [128, HPC * P], BF16)
    htmp = sb("htmp", [128, HPC * P], F32)
    carry = sb("carry", [1, TW], F32)
    zrow = sb("zrow", [1, 16], F32)
    rowtmp = sb("rowtmp", [1, 128], F32)

    def db(name, shape, dt):
        return [sb(name + str(i), shape, dt) for i in range(2)]

    xb = db("xb", [128, DK * 128], BF16)
    Xb = db("Xb", [128, XW], BF16)
    szb = db("szb", [128, XW], BF16)
    Bro = db("Bro", [128, BW], BF16)
    Cro = db("Cro", [128, BW], BF16)
    tiny = db("tiny", [128, 8 * HPC], F32)
    SL = {nm: idx * HPC for idx, nm in enumerate(["dt", "lam", "g", "gb", "s", "rdt", "c", "w"])}
    E8 = db("E8", [128, HPC], F32)
    scp = db("scp", [128, HPC], F32)
    web = db("web", [128, HPC], F32)
    yzT = db("yzT", [128, XW], BF16)

    xsq = sb("xsq", [128, DK * 128], BF16)
    st_m = sb("st_m", [1, 128], F32)
    st_q = sb("st_q", [1, 128], F32)
    st_r = [sb("st_r0", [1, 128], F32), sb("st_r1", [1, 128], F32)]
    st_rmn = [sb("st_rmn0", [1, 128], BF16), sb("st_rmn1", [1, 128], BF16)]
    rcolp = [sb("rcol0", [128, 1], F32), sb("rcol1", [128, 1], F32)]
    rsqcolp = [sb("rsqcol0", [128, 1], F32), sb("rsqcol1", [128, 1], F32)]
    rhcolp = [sb("rhcol0", [128, 1], F32), sb("rhcol1", [128, 1], F32)]
    eb = sb("eb", [128, XW], BF16)
    Braw = sb("Braw", [128, BW], BF16)
    Craw = sb("Craw", [128, BW], BF16)
    sqs = sb("sqs", [128, 1024], BF16)
    ropescr = sb("ropescr", [128, TW], BF16)
    q2B = sb("q2B", [128, HPC], F32)
    q2C = sb("q2C", [128, HPC], F32)
    rsB = sb("rsB", [128, HPC], F32)
    rsC = sb("rsC", [128, HPC], F32)
    sm1 = sb("sm1", [128, HPC], F32)
    sm2 = sb("sm2", [128, HPC], F32)
    sm3 = sb("sm3", [128, HPC], F32)
    sm4 = sb("sm4", [128, HPC], F32)
    lb = sb("lb", [128, HPC], F32)
    sth = sb("sth", [128, TW], F32)
    thk = sb("thk", [128, TW], F32)
    thp = sb("thp", [128, TW], F32)
    sinb = sb("sinb", [128, TW], BF16)
    cosb = sb("cosb", [128, TW], BF16)
    rhsall = sb("rhsall", [128, HPC * 128], BF16)
    expd = sb("expd", [128, HPC * 128], BF16)
    Bd = sb("Bd", [128, BW], BF16)
    Cd = sb("Cd", [128, BW], BF16)
    SWc = sb("SWc", [128, HPC * 128], BF16)
    SW = sb("SW", [128, HPC * 128], BF16)
    bcb = sb("bcb", [128, BW], BF16)
    sttb = sb("sttb", [128, HPC], F32)
    dsttb = sb("dsttb", [128, HPC], F32)
    ydg = sb("ydg", [128, XW], BF16)
    Bw = sb("Bw", [128, BW], BF16)
    ya = sb("ya", [128, XW], F32)
    yzb = sb("yzb", [128, XW], BF16)

    pj0, pj1 = psb("pj0"), psb("pj1")
    pcum, pell, psm, pdh, pa, pb = psb("pcum"), psb("pell"), psb("psm"), psb("pdh"), psb("pa"), psb("pb")
    PJ = [pj0, pj1]
    pdh_b = pdh.ap().bitcast(BF16)
    pa_b = pa.ap().bitcast(BF16)
    pb_b = pb.ap().bitcast(BF16)

    for dst, src in [(ul, c_ul), (astr, c_astr), (s1m, c_s1), (m01, c_m01),
                     (onesf, c_onesf), (onesr, c_onesr), (e127, c_e127),
                     (idb, c_idb), (csc, c_sc)]:
        em("Y", lambda e, d=dst, s=src: e.dma_start(out=d[:, :], in_=s.ap()),
           inc="dC", n=16)
    t_oc = em("V", lambda e: e.memset(onescol[:, :], 1.0), inc="V")
    em("V", lambda e: e.memset(onescolb[:, :], 1.0), inc="V")
    em("V", lambda e: e.memset(hall[:, :], 0.0), inc="V")
    em("V", lambda e: e.memset(hallb[:, :], 0.0), inc="V")
    em("V", lambda e: e.memset(carry[:, :], 0.0), inc="V")
    t_zr = em("V", lambda e: e.memset(zrow[:, :], 0.0), inc="V")

    nsplit = 4 if DK % 4 == 0 else 1
    kstep = DK // nsplit
    for part in range(nsplit):
        em("P", lambda e, part=part: e.dma_start(
            out=wcs[:, :].rearrange("p (k c) -> p k c", c=WROWS)[:, part * kstep:(part + 1) * kstep],
            in_=io["WcsT"].rearrange("(k p) c -> p k c", p=128)[:, part * kstep:(part + 1) * kstep]),
            inc="dW", n=16)
    t_wcs = k.now("dW")
    ld_row = sb("ld_row", [1, max(HPC * N, 128)], F32)
    row_free = [None]

    def ld_param(nm):
        t = em("Y", lambda e, nm=nm, w=int(io[nm].shape[1]): e.dma_start(
            out=ld_row[:, 0:w], in_=io[nm]), waits=[row_free[0]], inc="dR", n=16)
        return t
    t_const = k.now("dC")

    def ld_x(e, inst, par):
        return e.dma_start(
            out=xb[par][:, :].rearrange("p (k t) -> p k t", t=Q),
            in_=io["xT"].rearrange("(k p) t -> p k t", p=128)[:, :, inst * Q:(inst + 1) * Q])
    xload = {0: em("P", lambda e: ld_x(e, 0, 0), inc="dX0", n=16)}

    t_ld = ld_param("Alog")
    t1 = em("S", lambda e: e.activation(rowtmp[:, 0:HPC], ld_row[:, 0:HPC], AF.Exp),
            waits=[t_ld], inc="S")
    row_free[0] = t1
    t2 = em("V", lambda e: e.tensor_scalar(rowtmp[:, 0:HPC], rowtmp[:, 0:HPC], -1.0, None, ALU.mult),
            waits=[t1], inc="V")
    t3 = em("T", lambda e: e.matmul(psm[:, 0:HPC], onesr[:, :], rowtmp[:, 0:HPC],
                                    start=True, stop=True),
            waits=[t2, t_const], inc="T")
    t_arep = em("V", lambda e: e.tensor_copy(arep[:, :], psm[:, 0:HPC]), waits=[t3], inc="V")

    rep_prev = [t_arep]

    def rep_from_row(nm, width, dstbuf):
        t_ld = ld_param(nm)
        tlast = None
        for c in range((width + 511) // 512):
            w = min(512, width - 512 * c)
            tt = em("T", lambda e, c=c, w=w: e.matmul(
                psm[:, 0:w], onesr[:, :], ld_row[:, 512 * c:512 * c + w],
                start=True, stop=True),
                waits=[t_ld] + rep_prev, inc="T")
            tlast = em("V", lambda e, c=c, w=w, d=dstbuf: e.tensor_copy(
                d[:, 512 * c:512 * c + w], psm[:, 0:w]), waits=[tt], inc="V")
            rep_prev[:] = [tlast]
        row_free[0] = tt
        return tlast

    rep_from_row("Bnw", N, wbrep)
    rep_from_row("Cnw", N, wcrep)
    rep_from_row("Bb", HPC * N, bbrep)
    t_reps = rep_from_row("Cb", HPC * N, cbrep)
    t_par = t_reps

    ws_free = {}
    for ci, (co, cw) in enumerate(cfg.chunks):
        bank = PJ[ci % 2]
        for kk in range(DK):
            em("T", lambda e, kk=kk, co=co, cw=cw, bank=bank: e.matmul(
                bank[0:1, 0:cw], onescolb[:, :],
                wcs[:, kk * WROWS + co:kk * WROWS + co + cw],
                start=(kk == 0), stop=(kk == DK - 1)),
                waits=([t_wcs, t_oc, t_reps, ws_free.get(ci - 2)]) if kk == 0 else [],
                inc="T" if kk == DK - 1 else None)
        tt = k.now("T")
        ws_free[ci] = em("V", lambda e, co=co, cw=cw, bank=bank: e.tensor_copy(
            wsum[:, co:co + cw], bank[0:1, 0:cw]), waits=[tt], inc="V")
    t_wsum = ws_free[cfg.NCH - 1]

    def sec_chunks(name):
        o, w = cfg.sec[name]
        out = []
        for ci2, (co2, cw2) in enumerate(cfg.chunks):
            if co2 >= o and co2 < o + w:
                out.append((ci2, co2 - o, cw2))
        return out

    end_prev = [k.now(s) for s in ("T", "V", "S", "P")]

    store_tags = {}
    st = {}

    def emit_stats(i2, prev_l2):
        p2 = i2 % 2
        t_sq = em("S", lambda e: e.activation(xsq[:, :], xb[p2][:, :], AF.Square),
                  waits=[xload[i2], st.get(i2 - 1, {}).get("T")], inc="S")
        for kk in range(DK):
            em("T", lambda e, kk=kk: e.matmul(
                psm[0:1, 0:128], onescolb[:, :], xb[p2][:, kk * 128:(kk + 1) * 128],
                start=(kk == 0), stop=False),
                waits=[xload[i2], t_wsum, st.get(i2 - 1, {}).get("rc")] if kk == 0 else [])
        for kk in range(DK):
            em("T", lambda e, kk=kk: e.matmul(
                psm[0:1, 128:256], onescolb[:, :], xsq[:, kk * 128:(kk + 1) * 128],
                start=False, stop=(kk == DK - 1)),
                waits=[t_sq] if kk == 0 else [],
                inc="T" if kk == DK - 1 else None)
        t_stats = k.now("T")
        t_r1 = em("V", lambda e: e.tensor_scalar(st_m[:, :], psm[0:1, 0:128],
                                                 1.0 / cfg.D, None, ALU.mult),
                  waits=[t_stats], inc="V")
        t_r2 = em("V", lambda e: e.tensor_scalar(st_q[:, :], psm[0:1, 128:256],
                                                 1.0 / cfg.D, None, ALU.mult),
                  waits=[t_stats], inc="V")
        t_r3 = em("V", lambda e: e.tensor_tensor(st_r[p2][:, :], st_m[:, :], st_m[:, :], ALU.mult),
                  waits=[t_r1], inc="V")
        t_r4 = em("V", lambda e: e.tensor_tensor(st_r[p2][:, :], st_q[:, :], st_r[p2][:, :], ALU.subtract),
                  waits=[t_r2, t_r3], inc="V")
        t_r5 = em("S", lambda e: e.activation(st_r[p2][:, :], st_r[p2][:, :], AF.Ln,
                                              bias=csc[0:1, I_EPSLN:I_EPSLN + 1]),
                  waits=[t_r4], inc="S")
        t_r6 = em("S", lambda e: e.activation(st_r[p2][:, :], st_r[p2][:, :], AF.Exp, scale=-0.5),
                  waits=[t_r5], inc="S")
        t_r7 = em("V", lambda e: e.tensor_scalar(st_rmn[p2][:, :], st_m[:, :], -1.0, None, ALU.mult),
                  waits=[t_r1], inc="V")
        t_rc1 = em("T", lambda e: e.matmul(psm[:, 256:257], st_r[p2][:, :], onesr[:, 0:1],
                                           start=True, stop=True),
                   waits=[t_r6], inc="T")
        t_rc = em("V", lambda e: e.tensor_copy(rcolp[p2][:, :], psm[:, 256:257]),
                  waits=[t_rc1, prev_l2], inc="V")
        t_rsq = em("V", lambda e: e.tensor_tensor(rsqcolp[p2][:, :], rcolp[p2][:, :],
                                                  rcolp[p2][:, :], ALU.mult),
                   waits=[t_rc], inc="V")
        t_rh = em("V", lambda e: e.tensor_scalar(rhcolp[p2][:, :], rcolp[p2][:, :],
                                                 0.5, None, ALU.mult),
                  waits=[t_rc], inc="V")
        st[i2] = dict(T=t_stats, sq=t_sq, rc=t_rc, rsq=t_rsq, rh=t_rh, r7=t_r7)

    pv = {}
    for it in range(n_inst + 1):
        i, j = it, it - 1
        pi, pjy = i % 2, j % 2
        have_i = i < n_inst
        have_j = j >= 0
        if it == 0 or not have_i:
            k.fence(end_prev)

        if have_j:
            tj = lambda nm: tiny[pjy][:, SL[nm]:SL[nm] + HPC]
            hpb = (HPC + 1) // 2
            t_rhs = em("V", lambda e, a=bcast_h(tj("s"), HPC, 128): e.tensor_tensor(
                v3(rhsall[:, :], HPC, 128), a,
                bcast_rep(ul[:, :], HPC, 128), ALU.mult),
                waits=[pv.get("dm")], inc="V")

        if have_i:
            b_i, q_i = divmod(i, cfg.NQ)
            if it == 0:
                emit_stats(0, None)
            sti = st[i]
            rcol, rsqcol, rhcol = rcolp[pi], rsqcolp[pi], rhcolp[pi]
            t_rc, t_rsq, t_rh, t_r7 = sti["rc"], sti["rsq"], sti["rh"], sti["r7"]
            if i + 1 < n_inst:
                xload[i + 1] = em("P", lambda e, i=i: ld_x(e, i + 1, (i + 1) % 2),
                                  waits=[pv.get("projlast"), st.get(i - 1, {}).get("sq")],
                                  inc="dX%d" % ((i + 1) % 2), n=16)
            proj = {}
            bank_free = {}
            for ci, (co, cw) in enumerate(cfg.chunks):
                bank = PJ[ci % 2]
                w0 = [xload[i]]
                if ci == 0:
                    w0.append(pv.get("sth"))
                elif ci == 1:
                    w0 += [pv.get("u1"), pv.get("l2")]
                if ci - 2 >= 0:
                    w0.append(bank_free.get(ci - 2))
                for kk in range(DK):
                    em("T", lambda e, kk=kk, co=co, cw=cw, bank=bank, pi=pi: e.matmul(
                        bank[:, 0:cw], xb[pi][:, kk * 128:(kk + 1) * 128],
                        wcs[:, kk * WROWS + co:kk * WROWS + co + cw],
                        start=(kk == 0), stop=False),
                        waits=w0 if kk == 0 else [])
                em("T", lambda e, co=co, cw=cw, bank=bank, pi=pi: e.matmul(
                    bank[:, 0:cw], st_rmn[pi][:, :], wsum[:, co:co + cw],
                    start=False, stop=True), waits=[t_r7], inc="T")
                proj[ci] = k.now("T")

                name = None
                for nm in ["xp", "z", "B", "C", "th", "dt"]:
                    o, w = cfg.sec[nm]
                    if o <= co < o + w:
                        name = nm
                        so = co - o
                        break
                if name == "xp":
                    t = em("S", lambda e, bank=bank, so=so, cw=cw, pi=pi, rcol=rcol: e.activation(
                        Xb[pi][:, so:so + cw], bank[:, 0:cw], AF.Identity,
                        scale=rcol[:, :]), waits=[proj[ci], t_rc], inc="S")
                    bank_free[ci] = t
                elif name == "z":
                    t_zcp = em("V", lambda e, bank=bank, so=so, cw=cw: e.tensor_copy(
                        ropescr[:, so:so + cw], bank[:, 0:cw]),
                        waits=[proj[ci], pv.get("rope")], inc="V")
                    z_so, z_cw = so, cw
                    bank_free[ci] = t_zcp
                elif name in ("B", "C"):
                    raw = Braw if name == "B" else Craw
                    t_cp = em("V", lambda e, bank=bank, so=so, cw=cw, raw=raw: e.tensor_copy(
                        raw[:, so:so + cw], bank[:, 0:cw]),
                        waits=[proj[ci], pv.get("rope")], inc="V")
                    bank_free[ci] = t_cp
                elif name == "th":
                    bank_free[ci] = None
                    th_bank = bank
                    t_thp = proj[ci]
                elif name == "dt":
                    bank_free[ci] = None
                    dt_bank = bank
                    t_dtp = proj[ci]

            def bc_chain(name, raw, q2, rs, wrep, brep, outb):
                hlist = sec_chunks(name)
                tq = []
                for (ci2, so, cw2) in hlist:
                    nh = cw2 // N
                    h0 = so // N
                    ts1 = em("S", lambda e, so=so, cw2=cw2, raw=raw: e.activation(
                        sqs[:, so:so + cw2], raw[:, so:so + cw2], AF.Square),
                        waits=[bank_free[ci2]], inc="S")
                    tq.append(em("V", lambda e, so=so, cw2=cw2, nh=nh, h0=h0, q2=q2: e.tensor_reduce(
                        q2[:, h0:h0 + nh].unsqueeze(2),
                        sqs[:, so:so + cw2].rearrange("p (h n) -> p h n", h=nh),
                        AX.X, ALU.add), waits=[ts1], inc="V"))
                t_ms = em("V", lambda e, q2=q2, rsqcol=rsqcol: e.tensor_scalar(
                    q2[:, :], q2[:, :], rsqcol[:, :], 1.0 / N, ALU.mult, ALU.mult),
                    waits=tq + [t_rsq], inc="V")
                t_l = em("S", lambda e, q2=q2: e.activation(
                    q2[:, :], q2[:, :], AF.Ln, bias=csc[:, I_EPSRMS:I_EPSRMS + 1]),
                    waits=[t_ms], inc="S")
                t_e2 = em("S", lambda e, q2=q2: e.activation(
                    q2[:, :], q2[:, :], AF.Exp, scale=-0.5), waits=[t_l], inc="S")
                t_rs = em("V", lambda e, q2=q2, rs=rs, rcol=rcol: e.tensor_scalar(
                    rs[:, :], q2[:, :], rcol[:, :], None, ALU.mult), waits=[t_e2], inc="V")
                def emit_value(extra):
                    w1 = [t_rs] + [bank_free[c2] for (c2, _, _) in hlist] + extra
                    t_v1 = []
                    for h in range(HPC):
                        t_v1.append(em("S", lambda e, h=h, raw=raw, rs=rs: e.activation(
                            raw[:, h * N:(h + 1) * N], raw[:, h * N:(h + 1) * N],
                            AF.Identity, scale=rs[:, h:h + 1]),
                            waits=w1 if h == 0 else [], inc="S"))
                    t_v2 = em("V", lambda e, wrep=wrep: e.tensor_tensor(
                        v3(outb[:, :], HPC, N), v3(raw[:, :], HPC, N),
                        bcast_rep(wrep[:, :], HPC, N), ALU.mult), waits=t_v1, inc="V")
                    t_v3 = em("V", lambda e, brep=brep: e.tensor_tensor(
                        outb[:, :], outb[:, :], brep[:, :], ALU.add), waits=[t_v2], inc="V")
                    return t_v3
                return emit_value

            fB = bc_chain("B", Braw, q2B, rsB, wbrep, bbrep, Braw)
            fC = bc_chain("C", Craw, q2C, rsC, wcrep, cbrep, Craw)

            dcol = lambda nm: tiny[pi][:, SL[nm]:SL[nm] + HPC]
            t_u1 = em("V", lambda e, db_=dt_bank, rcol=rcol: e.tensor_scalar(
                sm1[:, :], db_[:, 0:HPC], rcol[:, :], None, ALU.mult),
                waits=[t_dtp, t_rc, pv.get("we")], inc="V")
            t_u2 = em("S", lambda e: e.activation(sm2[:, :], sm1[:, :], AF.Abs), waits=[t_u1], inc="S")
            t_u3 = em("S", lambda e: e.activation(sm2[:, :], sm2[:, :], AF.Exp, scale=-1.0),
                      waits=[t_u2], inc="S")
            t_u4 = em("S", lambda e: e.activation(sm2[:, :], sm2[:, :], AF.Ln,
                                                  bias=csc[:, I_ONE:I_ONE + 1]),
                      waits=[t_u3], inc="S")
            t_u5 = em("V", lambda e: e.tensor_scalar(sm4[:, :], sm1[:, :], 0.0, None, ALU.max),
                      waits=[t_u1], inc="V")
            t_dt = em("V", lambda e, a=dcol("dt"): e.tensor_tensor(a, sm4[:, :], sm2[:, :], ALU.add),
                      waits=[t_u4, t_u5], inc="V")
            t_rdt = em("V", lambda e, a=dcol("rdt"), b=dcol("dt"), rcol=rcol: e.tensor_scalar(a, b, rcol[:, :], None, ALU.mult),
                       waits=[t_dt], inc="V")
            t_sth = em("V", lambda e, a=bcast_h(dcol("rdt"), HPC, N // 2), tb=th_bank: e.tensor_tensor(
                v3(sth[:, :], HPC, N // 2), v3(tb[:, 0:TW], HPC, N // 2),
                a, ALU.mult),
                waits=[t_thp, t_rdt], inc="V")
            t_l2 = em("S", lambda e, db_=dt_bank, rhcol=rhcol: e.activation(
                sm3[:, :], db_[:, HPC:2 * HPC], AF.Tanh, scale=rhcol[:, :]),
                waits=[t_dtp, t_rh], inc="S")
            t_lam = em("V", lambda e, a=dcol("lam"): e.tensor_scalar(
                a, sm3[:, :], 0.5, 0.5, ALU.mult, ALU.add), waits=[t_l2], inc="V")
            t_e = em("S", lambda e, so=z_so, cw=z_cw, rhcol=rhcol: e.activation(
                eb[:, so:so + cw], ropescr[:, so:so + cw], AF.Tanh, scale=rhcol[:, :]),
                waits=[t_zcp, t_rh], inc="S")
            t_zh = em("V", lambda e, so=z_so, cw=z_cw, rhcol=rhcol: e.tensor_scalar(
                thp[:, so:so + cw], ropescr[:, so:so + cw], rhcol[:, :], None, ALU.mult),
                waits=[t_zcp, t_rh], inc="V")
            t_z = em("V", lambda e, so=z_so, cw=z_cw, pi=pi: e.scalar_tensor_tensor(
                szb[pi][:, so:so + cw], eb[:, so:so + cw], 1.0,
                thp[:, so:so + cw], ALU.add, ALU.mult),
                waits=[t_e, t_zh], inc="V")
            t_g = em("V", lambda e, a=dcol("g"), b=dcol("dt"), c=dcol("lam"): e.tensor_tensor(a, b, c, ALU.mult),
                     waits=[t_dt, t_lam], inc="V")
            t_gb = em("V", lambda e, a=dcol("gb"), b=dcol("dt"), c=dcol("g"): e.tensor_tensor(a, b, c, ALU.subtract),
                      waits=[t_g], inc="V")
            t_s = em("V", lambda e, a=dcol("s"), b=dcol("dt"): e.tensor_tensor(a, b, arep[:, :], ALU.mult),
                     waits=[t_dt, t_arep], inc="V")

        if have_j:
            t_dm = []
            for h in range(HPC):
                bank = pa if h < hpb else pb
                off = (h % hpb) * 128
                last = (h % hpb == hpb - 1 or h == HPC - 1)
                em("T", lambda e, h=h, bank=bank, off=off: e.matmul(
                    bank[:, off:off + 128], astr[:, :],
                    rhsall[:, h * 128:(h + 1) * 128],
                    start=(h % hpb == 0), stop=False),
                    waits=[t_rhs, pv.get("yb")] if h == 0 else [])
                em("T", lambda e, h=h, bank=bank, off=off, last=last: e.matmul(
                    bank[:, off:off + 128], m01[:, :], idb[:, :],
                    start=False, stop=last),
                    inc="T" if last else None)
                if last:
                    t_dm.append(k.now("T"))
            t_x1 = em("S", lambda e: e.activation(expd[:, 0:hpb * 128],
                                                  pa[:, 0:hpb * 128], AF.Exp),
                      waits=[t_dm[0]], inc="S")
            t_x2 = em("S", lambda e: e.activation(expd[:, hpb * 128:HPC * 128],
                                                  pb[:, 0:(HPC - hpb) * 128], AF.Exp),
                      waits=[t_dm[-1]], inc="S")

            for h in range(HPC):
                bank = pa_b if h < hpb else pb_b
                off = (h % hpb) * 256
                em("T", lambda e, h=h, bank=bank, off=off, pjy=pjy: e.transpose(
                    bank[:, off:off + 128], Bro[pjy][:, h * N:h * N + 128], idb[:, :]),
                    waits=[t_x1, t_x2, pv.get("rope")] if h == 0 else [])
                em("T", lambda e, h=h, bank=bank, off=off, pjy=pjy: e.transpose(
                    bank[:, off + 128:off + 256], Cro[pjy][:, h * N:h * N + 128], idb[:, :]),
                    inc="T" if h == HPC - 1 else None)
            t_tr = k.now("T")
            t_bd1 = em("V", lambda e: e.tensor_copy(
                v3(Bd[:, 0:hpb * 128], hpb, 128),
                pa_b[:, 0:hpb * 256].rearrange("p (h u n) -> p u h n", u=2, n=128)[:, 0]),
                waits=[t_tr], inc="V")
            t_bd2 = em("V", lambda e: e.tensor_copy(
                v3(Bd[:, hpb * 128:HPC * 128], HPC - hpb, 128),
                pb_b[:, 0:(HPC - hpb) * 256].rearrange("p (h u n) -> p u h n", u=2, n=128)[:, 0]),
                waits=[t_tr], inc="V")
            t_cd1 = em("V", lambda e: e.tensor_copy(
                v3(Cd[:, 0:hpb * 128], hpb, 128),
                pa_b[:, 0:hpb * 256].rearrange("p (h u n) -> p u h n", u=2, n=128)[:, 1]),
                waits=[t_tr], inc="V")
            t_cd2 = em("V", lambda e: e.tensor_copy(
                v3(Cd[:, hpb * 128:HPC * 128], HPC - hpb, 128),
                pb_b[:, 0:(HPC - hpb) * 256].rearrange("p (h u n) -> p u h n", u=2, n=128)[:, 1]),
                waits=[t_tr], inc="V")
            t_bcd = [t_bd1, t_bd2, t_cd1, t_cd2]


        if have_i:
            em("T", lambda e: e.matmul(pcum[:, 0:TW], ul[:, :], sth[:, 0:TW],
                                       start=True, stop=False),
               waits=[t_sth, t_const, pv.get("th"), pv.get("cu")])
            t_cum = em("T", lambda e: e.matmul(pcum[:, 0:TW], onesr[:, :], carry[:, 0:TW],
                                               start=False, stop=True), inc="T")
            t_tot = em("T", lambda e: e.matmul(pdh[0:1, 0:TW], onesf[:, 0:1], sth[:, 0:TW],
                                               start=True, stop=True),
                       waits=[pv.get("ycp")], inc="T")
            t_ell = em("T", lambda e, a=dcol("s"): e.matmul(pell[:, 0:HPC], ul[:, :], a,
                                               start=True, stop=True),
                       waits=[t_s, pv.get("c"), pv.get("sc")], inc="T")
            t_st2 = em("T", lambda e, a=dcol("s"): e.matmul(pell[:, HPC:2 * HPC], onesf[:, :], a,
                                               start=True, stop=True), inc="T")
            if q_i == cfg.NQ - 1:
                t_cu = em("V", lambda e: e.memset(carry[:, :], 0.0), waits=[t_cum], inc="V")
            else:
                t_cu = em("V", lambda e: e.tensor_tensor(
                    carry[:, 0:TW], carry[:, 0:TW], pdh[0:1, 0:TW], ALU.add),
                    waits=[t_tot, t_cum], inc="V")
            t_lb = em("V", lambda e: e.tensor_copy(lb[:, :], pell[:, 0:HPC]), waits=[t_st2], inc="V")
            t_E8 = em("S", lambda e, pi=pi: e.activation(E8[pi][:, :], pell[:, 0:HPC], AF.Exp),
                      waits=[t_st2, pv.get("h3")], inc="S")
            t_sc = em("S", lambda e, pi=pi: e.activation(scp[pi][:, :], pell[:, HPC:2 * HPC], AF.Exp),
                      waits=[t_st2, pv.get("h3")], inc="S")
            t_d1 = em("V", lambda e: e.tensor_tensor(sm1[:, :], pell[:, HPC:2 * HPC], lb[:, :],
                                                     ALU.subtract),
                      waits=[t_st2, t_lb], inc="V")
            t_we = em("S", lambda e, pi=pi: e.activation(web[pi][:, :], sm1[:, :], AF.Exp),
                      waits=[t_d1], inc="S")

            t_k1 = em("V", lambda e: e.tensor_scalar(thk[:, :], pcum[:, 0:TW],
                                                     INV_2PI, MAGIC, ALU.mult, ALU.add),
                      waits=[t_cum, pv.get("cos")], inc="V")
            t_k2 = em("V", lambda e: e.tensor_scalar(thk[:, :], thk[:, :], MAGIC, None, ALU.subtract),
                      waits=[t_k1], inc="V")
            t_th = em("V", lambda e: e.scalar_tensor_tensor(
                thp[:, :], thk[:, :], -TWO_PI, pcum[:, 0:TW], ALU.mult, ALU.add),
                waits=[t_k2], inc="V")
            t_sin = em("S", lambda e: e.activation(sinb[:, :], thp[:, :], AF.Sin),
                       waits=[t_th, pv.get("rope")], inc="S")
            t_ab = em("S", lambda e: e.activation(thk[:, :], thp[:, :], AF.Abs), waits=[t_th], inc="S")
            t_cos = em("S", lambda e: e.activation(cosb[:, :], thk[:, :], AF.Sin,
                                                   scale=-1.0, bias=csc[:, I_HALFPI:I_HALFPI + 1]),
                       waits=[t_ab], inc="S")

            def rope2(src, dst, t_src, par, prev):
                sv = lambda b, h0, h1: b[:, :].rearrange("p (h n) -> p h n", h=HPC)[:, :, h0:h1]
                cs = v3(cosb[:, :], HPC, N // 2)
                sn = v3(sinb[:, :], HPC, N // 2)
                scr = v3(ropescr[:, :], HPC, N // 2)
                t1 = em("V", lambda e: e.tensor_tensor(scr, sv(src, N // 2, N), sn, ALU.mult),
                        waits=[t_src, t_sin, prev, t_z, pv.get("tr"), pv.get("bw")], inc="V")
                t2 = em("V", lambda e: e.tensor_tensor(sv(dst[par], 0, N // 2),
                                                       sv(src, 0, N // 2), cs, ALU.mult),
                        waits=[t_cos], inc="V")
                t3 = em("V", lambda e: e.tensor_tensor(sv(dst[par], 0, N // 2),
                                                       sv(dst[par], 0, N // 2), scr, ALU.subtract),
                        waits=[t1, t2], inc="V")
                t4 = em("V", lambda e: e.tensor_tensor(scr, sv(src, N // 2, N), cs, ALU.mult),
                        waits=[t3], inc="V")
                t5 = em("V", lambda e: e.tensor_tensor(sv(dst[par], N // 2, N),
                                                       sv(src, 0, N // 2), sn, ALU.mult),
                        inc="V")
                t6 = em("V", lambda e: e.tensor_tensor(sv(dst[par], N // 2, N),
                                                       sv(dst[par], N // 2, N), scr, ALU.add),
                        waits=[t4, t5], inc="V")
                return t6


        if have_j:
            b_j, q_j = divmod(j, cfg.NQ)

            for h in range(HPC):
                bank = pa if h < hpb else pb
                off = (h % hpb) * 128
                em("T", lambda e, h=h, bank=bank, off=off: e.matmul(
                    bank[:, off:off + 128], Bd[:, h * 128:(h + 1) * 128],
                    Cd[:, h * 128:(h + 1) * 128], start=True, stop=True),
                    waits=t_bcd if h == 0 else [],
                    inc="T" if (h == hpb - 1 or h == HPC - 1) else None)
            t_st = k.now("T")

            gnext_row = (tiny[pi][0:1, SL["gb"]:SL["gb"] + HPC] if have_i
                         else zrow[0:1, 0:HPC])
            cm_w = [t_zr, pv.get("c")]
            if have_i:
                cm_w += [t_gb, t_d1, t_sc]
            em("T", lambda e, a=tj("gb"): e.matmul(pell[:, 2 * HPC:3 * HPC], s1m[:, :], a,
                                       start=True, stop=False), waits=cm_w)
            t_cm = em("T", lambda e, g=gnext_row: e.matmul(
                pell[:, 2 * HPC:3 * HPC], e127[:, :], g, start=False, stop=True), inc="T")
            t_c = em("V", lambda e, a=tj("c"), b=tj("g"): e.tensor_tensor(a, pell[:, 2 * HPC:3 * HPC],
                                                    b, ALU.add), waits=[t_cm], inc="V")
            t_w = em("V", lambda e, a=tj("w"), b=tj("c"), pjy=pjy: e.tensor_tensor(a, web[pjy][:, :], b, ALU.mult),
                     waits=[t_c], inc="V")

            t_swc1 = em("V", lambda e, a=bcast_h(tj("c")[:, 0:hpb], hpb, 128): e.tensor_tensor(
                v3(SWc[:, 0:hpb * 128], hpb, 128),
                pa[:, 0:hpb * 128].rearrange("p (h n) -> p h n", h=hpb),
                a, ALU.mult),
                waits=[t_st, t_c], inc="V")
            t_swc2 = em("V", lambda e, a=bcast_h(tj("c")[:, hpb:HPC], HPC - hpb, 128): e.tensor_tensor(
                v3(SWc[:, hpb * 128:HPC * 128], HPC - hpb, 128),
                pb[:, 0:(HPC - hpb) * 128].rearrange("p (h n) -> p h n", h=HPC - hpb),
                a, ALU.mult),
                waits=[t_st], inc="V")
            t_sw = em("V", lambda e: e.tensor_tensor(SW[:, :], SWc[:, :], expd[:, :], ALU.mult),
                      waits=[t_swc1, t_swc2, t_x1, t_x2], inc="V")

            t_bc = em("V", lambda e, pjy=pjy: e.tensor_tensor(
                bcb[:, :], Bro[pjy][:, :], Cro[pjy][:, :], ALU.mult),
                waits=[pv.get("rope")], inc="V")
            t_stt = em("V", lambda e: e.tensor_reduce(
                sttb[:, :].unsqueeze(2), v3(bcb[:, :], HPC, N), AX.X, ALU.add),
                waits=[t_bc], inc="V")
            t_dstt = em("V", lambda e, a=tj("g"): e.tensor_tensor(dsttb[:, :], a, sttb[:, :],
                                                       ALU.mult), waits=[t_stt], inc="V")
            t_ydg = em("V", lambda e, pjy=pjy: e.tensor_tensor(
                v3(ydg[:, :], HPC, P), v3(Xb[pjy][:, :], HPC, P),
                bcast_h(dsttb[:, :], HPC, P), ALU.mult), waits=[t_dstt], inc="V")

            t_bw = em("V", lambda e, pjy=pjy, a=bcast_h(tj("w"), HPC, N): e.tensor_tensor(
                v3(Bw[:, :], HPC, N), v3(Bro[pjy][:, :], HPC, N),
                a, ALU.mult), waits=[t_w], inc="V")

            for h in range(HPC):
                em("T", lambda e, h=h, pjy=pjy: e.matmul(
                    pa[:, h * P:(h + 1) * P], SW[:, h * 128:(h + 1) * 128],
                    Xb[pjy][:, h * P:(h + 1) * P], start=True, stop=True),
                    waits=[t_sw, t_swc1] if h == 0 else [],
                    inc="T" if h == HPC - 1 else None)
            t_yi = k.now("T")
            for h in range(HPC):
                em("T", lambda e, h=h: e.matmul(
                    pb[:, h * P:(h + 1) * P], Cd[:, h * 128:(h + 1) * 128],
                    hallb[:, h * P:(h + 1) * P], start=True, stop=True),
                    waits=[t_swc2, t_sw, pv.get("h3")] if h == 0 else [],
                    inc="T" if h == HPC - 1 else None)
            t_ye = k.now("T")
            for h in range(HPC):
                em("T", lambda e, h=h, pjy=pjy: e.matmul(
                    pdh[:, h * P:(h + 1) * P], Bw[:, h * N:(h + 1) * N],
                    Xb[pjy][:, h * P:(h + 1) * P], start=True, stop=True),
                    waits=[t_bw, pv.get("ycp")] if h == 0 else [],
                    inc="T" if h == HPC - 1 else None)
            t_dh = k.now("T")

            t_ya = em("V", lambda e, pjy=pjy: e.tensor_tensor(
                v3(ya[:, :], HPC, P), pa[:, 0:XW].rearrange("p (h n) -> p h n", h=HPC)
                if False else pb[:, 0:XW].rearrange("p (h n) -> p h n", h=HPC),
                bcast_h(E8[pjy][:, :], HPC, P), ALU.mult), waits=[t_ye], inc="V")
            t_yb = em("V", lambda e: e.tensor_tensor(ya[:, :], ya[:, :], pa[:, 0:XW], ALU.add),
                      waits=[t_ya, t_yi], inc="V")
            t_yc = em("V", lambda e: e.tensor_tensor(ya[:, :], ya[:, :], ydg[:, :], ALU.add),
                      waits=[t_yb, t_ydg], inc="V")
            t_yz = em("V", lambda e, pjy=pjy: e.tensor_tensor(
                yzb[:, :], ya[:, :], szb[pjy][:, :], ALU.mult), waits=[t_yc], inc="V")

            t_h1 = em("V", lambda e, pjy=pjy: e.tensor_tensor(
                v3(htmp[:, :], HPC, P), v3(hall[:, :], HPC, P),
                bcast_h(scp[pjy][:, :], HPC, P), ALU.mult), waits=[t_ye, t_dh], inc="V")
            if q_j == cfg.NQ - 1:
                t_h2 = em("V", lambda e: e.memset(hall[:, :], 0.0), waits=[t_h1], inc="V")
            else:
                t_h2 = em("V", lambda e: e.tensor_tensor(hall[:, :], htmp[:, :],
                                                         pdh[:, 0:XW], ALU.add),
                          waits=[t_h1], inc="V")
            t_h3 = em("V", lambda e: e.tensor_copy(hallb[:, :], hall[:, :]),
                      waits=[t_h2], inc="V")

            ntile = XW // 128
            for ti in range(ntile):
                em("T", lambda e, ti=ti: e.transpose(
                    pdh_b[:, ti * 128:(ti + 1) * 128], yzb[:, ti * 128:(ti + 1) * 128],
                    idb[:, :]),
                    waits=[t_yz, t_h2] if ti == 0 else [],
                    inc="T" if ti == ntile - 1 else None)
            t_yt = k.now("T")
            t_ycp = em("V", lambda e, pjy=pjy: e.tensor_copy(
                yzT[pjy][:, :], pdh_b[:, 0:XW]),
                waits=[t_yt, store_tags.get(j - 2)], inc="V")
            store_tags[j] = em("Y", lambda e, j=j, pjy=pjy: e.dma_start(
                out=io["yzS"].rearrange("(g p) t -> p g t", p=128)[:, :, j * Q:(j + 1) * Q],
                in_=yzT[pjy][:, :].rearrange("p (g t) -> p g t", t=Q)),
                waits=[t_ycp], inc="dY%d" % pjy, n=16)

        if have_i and i + 1 < n_inst:
            emit_stats(i + 1, pv.get("l2"))

        if have_i:
            t_bv = fB([])
            t_cv = fC([])
            t_brope = rope2(Braw, Bro, t_bv, pi, t_cv)
            t_crope = rope2(Craw, Cro, t_cv, pi, t_brope)

        if have_i:
            pv.update(projlast=proj[cfg.NCH - 1],
                      sth=t_sth, u1=t_u1, l2=t_l2, th=t_th, cu=t_cu,
                      cos=t_cos, rope=t_crope, we=t_we, sc=t_sc, c=t_d1,
                      d1=t_d1)
        if have_j:
            pv.update(c=t_c, dm=t_dm[-1], yb=t_yb, ycp=t_ycp, h3=t_h3,
                      tr=t_tr, bw=t_bw)
        end_prev = [k.now(s) for s in ("T", "V", "S", "P")]

    return [k.now("dY0"), k.now("dY1")]


def build_phaseA(cfg):
    nc = bass.Bass("TRN2", target_bir_lowering=False)
    k = K(nc)
    io = {}
    io["xT"] = nc.dram_tensor("xT", [cfg.D, cfg.Ltot], F32, kind="ExternalInput").ap()
    io["WcsT"] = nc.dram_tensor("WcsT", [cfg.D, cfg.WROWS], F32, kind="ExternalInput").ap()
    for nm, shape in [("Alog", [1, cfg.HPC]), ("Bnw", [1, cfg.N]), ("Cnw", [1, cfg.N]),
                      ("Bb", [1, cfg.HPC * cfg.N]), ("Cb", [1, cfg.HPC * cfg.N])]:
        io[nm] = nc.dram_tensor(nm, shape, F32, kind="ExternalInput").ap()
    io["yzS"] = nc.dram_tensor("yzS", [cfg.HPC * cfg.P, cfg.Ltot], BF16,
                               kind="ExternalOutput").ap()
    with contextlib.ExitStack() as stack:
        io["stack"] = stack
        t_done = emit_phaseA(k, cfg, io)
        k.emit("Y", None, waits=t_done)
        k.replay()
    return nc



def emit_phaseW(k, cfg, io):
    nc = k.nc
    em = k.emit
    HID = cfg.HID
    RROWS = cfg.Dout // cfg.ncores
    st = io["stack"]
    sb = lambda name, shape, dt: st.enter_context(nc.sbuf_tensor(name, shape, dt))
    wA = sb("wA", [128, HID], F32)
    wB = sb("wB", [128, HID], F32)
    wbf = sb("wbf", [128, HID], BF16)
    idwb = sb("idwb", [128, 128], BF16)
    tout = sb("tout", [128, 128], BF16)
    pw0 = st.enter_context(nc.psum_tensor("pw0", [128, 512], F32))
    pw0b = pw0.ap().bitcast(BF16)
    c_idw = nc.inline_tensor(np.eye(128, dtype=ml_dtypes.bfloat16), "c_idw")
    em("Y", lambda e: e.dma_start(out=idwb[:, :], in_=c_idw.ap()), inc="dC", n=16)
    t_id = k.now("dC")
    k.new_sem("dWF")
    k.new_sem("dWS")
    scale = 1.0 / math.sqrt(HID)
    ntile = (RROWS + 127) // 128
    prev_bf = None
    prev_cp = None
    prev_pv = None
    for tile in range(ntile):
        p0 = tile * 128
        pn = min(128, RROWS - p0)
        tl = em("Y", lambda e, p0=p0, pn=pn: e.dma_start(
            out=wA[0:pn, :], in_=io["W2s"][p0:p0 + pn, :]),
            waits=[prev_bf], inc="dWF", n=16)
        bufs = [wA, wB]
        t_st = tl
        nst = int(math.log2(HID))
        for si in range(nst):
            h = 1 << si
            src, dst = bufs[si % 2], bufs[(si + 1) % 2]
            va = lambda b, h=h, pn=pn: b[0:pn, :].rearrange("p (g u h) -> p g u h", u=2, h=h)
            em("V", lambda e, src=src, dst=dst, va=va: e.tensor_tensor(
                va(dst)[:, :, 0, :], va(src)[:, :, 0, :], va(src)[:, :, 1, :], ALU.add),
                waits=[t_st, prev_bf], inc="V")
            t_st = em("V", lambda e, src=src, dst=dst, va=va: e.tensor_tensor(
                va(dst)[:, :, 1, :], va(src)[:, :, 0, :], va(src)[:, :, 1, :], ALU.subtract),
                inc="V")
        fin = bufs[nst % 2]
        t_bf = em("V", lambda e, fin=fin, pn=pn: e.tensor_scalar(
            wbf[0:pn, :], fin[0:pn, :], scale, None, ALU.mult), waits=[t_st], inc="V")
        prev_bf = t_bf
        for hk in range(HID // 128):
            t_t = em("T", lambda e, hk=hk, pn=pn: e.transpose(
                pw0b[:, 0:pn], wbf[0:pn, hk * 128:(hk + 1) * 128], idwb[0:pn, 0:pn]),
                waits=[t_bf, t_id, prev_cp], inc="T")
            t_cp = em("V", lambda e, pn=pn: e.tensor_copy(tout[:, 0:pn], pw0b[:, 0:pn]),
                      waits=[t_t, prev_pv], inc="V")
            prev_pv = em("Y", lambda e, hk=hk, p0=p0, pn=pn: e.dma_start(
                out=io["W2hs"][hk * 128:(hk + 1) * 128, p0:p0 + pn], in_=tout[:, 0:pn]),
                waits=[t_cp], inc="dWS", n=16)
            prev_cp = t_cp
    return [k.now("dWS")]



def emit_phaseC(k, cfg, io):
    nc = k.nc
    em = k.emit
    D, HID, DIN, Dout = cfg.D, cfg.HID, cfg.DIN, cfg.Dout
    SPAN = cfg.SPAN
    DINK, DK, HK = DIN // 128, D // 128, HID // 128
    NT = SPAN // 128
    TPP = min(4, NT)
    NPASS = (NT + TPP - 1) // TPP
    RROWS = Dout // cfg.ncores
    g0 = 0
    DOC = Dout // 512 if Dout >= 512 else 1
    DCW = min(512, Dout)

    x1sp = nc.dram_tensor("x1sp", [SPAN, D], F32, kind="Internal").ap()
    xn2T = nc.dram_tensor("xn2T", [D, SPAN], BF16, kind="Internal").ap()

    for s in ["dz0", "dz1", "da0", "da1", "ds0", "ds1", "dn0", "dn1", "dwo",
              "dw10", "dw11", "dw30", "dw31", "du0", "du1", "do0", "do1",
              "dxp0", "dxp1", "dxr0", "dxr1"]:
        k.new_sem(s)

    st = io["stack"]
    sb = lambda name, shape, dt: st.enter_context(nc.sbuf_tensor(name, shape, dt))
    psb = lambda name: st.enter_context(nc.psum_tensor(name, [128, 512], F32))
    pbanks = [psb("pc%d" % i) for i in range(8)]
    csc2 = sb("csc2", [128, 1], F32)
    c_sc2 = nc.inline_tensor(np.full((128, 1), 1e-5, np.float32), "c_sc2")
    idc = sb("idc", [128, 128], BF16)
    c_idc = nc.inline_tensor(np.eye(128, dtype=ml_dtypes.bfloat16), "c_idc2")
    em("Y", lambda e: e.dma_start(out=csc2[:, :], in_=c_sc2.ap()),
       waits=list(io.get("ta_prev") or []), inc="dC", n=16)
    em("Y", lambda e: e.dma_start(out=idc[:, :], in_=c_idc.ap()), inc="dC", n=16)
    t_cl = k.now("dC")

    def c1():
        st1 = contextlib.ExitStack()
        sb1 = lambda name, shape, dt: st1.enter_context(nc.sbuf_tensor(name, shape, dt))
        wouts = sb1("wouts", [128, DINK * Dout], BF16)
        yzs = [sb1("yzs%d" % i, [128, DINK * 128], BF16) for i in range(2)]
        xtl = [sb1("xtl%d" % i, [128, D], F32) for i in range(2)]
        o1f = sb1("o1f", [128, D], F32)
        sqc = sb1("sqc", [128, D], BF16)
        x1f = sb1("x1f", [128, D], F32)
        xnb = sb1("xnb", [128, D], BF16)
        xnt = sb1("xnt", [128, D], BF16)
        mcol = sb1("mcol", [128, 1], F32)
        qcol = sb1("qcol", [128, 1], F32)
        vcol = sb1("vcol", [128, 1], F32)
        rcol2 = sb1("rcol2", [128, 1], F32)

        wosrc = io.get("Woutbf", io["WoutT"])
        for part in range(8):
            em("P", lambda e, part=part, wosrc=wosrc: e.dma_start(
                out=wouts[:, :].rearrange("p (kk c) -> p kk c", c=Dout)[:, part * (DINK // 8):(part + 1) * (DINK // 8)],
                in_=wosrc.rearrange("(kk p) c -> p kk c", p=128)[:, part * (DINK // 8):(part + 1) * (DINK // 8)]),
                waits=[io.get("t_cast")] if part == 0 else [], inc="dwo", n=16)
        t_wo = k.now("dwo")
        ldz, ldx = {}, {}

        yzv = io["yzsp"].rearrange("(kk p) t -> p kk t", p=128)
        if io.get("pid") is not None:
            yzv = yzv[:, :, bass.ts(io["pid"], SPAN)]

        def ld_tile(tt):
            par = tt % 2
            ldz[tt] = em("Y", lambda e, tt=tt, par=par: e.dma_start(
                out=yzs[par][:, :].rearrange("p (kk t) -> p kk t", t=128),
                in_=yzv[:, :, tt * 128:(tt + 1) * 128]),
                waits=[io.get("t_yzA")] if tt == 0 else [],
                inc="dz%d" % par, n=16)
            ldx[tt] = em("Y", lambda e, tt=tt, par=par: e.dma_start(
                out=xtl[par][:, :], in_=io["xsp"][tt * 128:(tt + 1) * 128, :]),
                inc="da%d" % par, n=16)

        ld_tile(0)
        s_store = {}
        n_store = {}
        prev_sq = {}
        prev_ev = {}
        prev_trans = {}
        for tt in range(NT):
            par = tt % 2
            if tt + 1 < NT:
                ld_tile(tt + 1)
            for dc in range(DOC):
                for kk in range(DINK):
                    em("T", lambda e, dc=dc, kk=kk, par=par: e.matmul(
                        pbanks[dc][:, 0:DCW], yzs[par][:, kk * 128:(kk + 1) * 128],
                        wouts[:, kk * Dout + dc * DCW:kk * Dout + (dc + 1) * DCW],
                        start=(kk == 0), stop=(kk == DINK - 1)),
                        waits=([ldz[tt], t_wo] + prev_ev.get(tt - 1, [])) if (kk == 0 and dc == 0) else [],
                        inc="T" if kk == DINK - 1 else None)
            t_mm = k.now("T")
            ev = []
            for dc in range(DOC):
                ev.append(em("V", lambda e, dc=dc: e.tensor_copy(
                    o1f[:, dc * DCW:(dc + 1) * DCW], pbanks[dc][:, 0:DCW]),
                    waits=[t_mm] + prev_sq.get(tt - 1, []), inc="V"))
            prev_ev[tt] = ev
            t_s1 = em("V", lambda e: e.tensor_reduce(mcol[:, :], o1f[:, :], AX.X, ALU.add),
                      waits=ev, inc="V")
            t_s2 = em("S", lambda e: e.activation(sqc[:, :], o1f[:, :], AF.Square,
                                                  accum_out=qcol[:, :]),
                      waits=ev + prev_sq.get(tt - 1, []), inc="S")
            t_s3 = em("V", lambda e: e.tensor_scalar(mcol[:, :], mcol[:, :], 1.0 / D, None,
                                                     ALU.mult), waits=[t_s1], inc="V")
            t_s4 = em("V", lambda e: e.tensor_tensor(vcol[:, :], mcol[:, :], mcol[:, :],
                                                     ALU.mult), waits=[t_s3], inc="V")
            t_s5 = em("V", lambda e: e.tensor_scalar(qcol[:, :], qcol[:, :], 1.0 / D, None,
                                                     ALU.mult), waits=[t_s2], inc="V")
            t_s6 = em("V", lambda e: e.tensor_tensor(vcol[:, :], qcol[:, :], vcol[:, :],
                                                     ALU.subtract), waits=[t_s4, t_s5], inc="V")
            t_s7 = em("S", lambda e: e.activation(vcol[:, :], vcol[:, :], AF.Ln,
                                                  bias=csc2[:, :]), waits=[t_s6, t_cl], inc="S")
            t_s8 = em("S", lambda e: e.activation(rcol2[:, :], vcol[:, :], AF.Exp, scale=-0.5),
                      waits=[t_s7], inc="S")
            t_n1 = em("V", lambda e: e.tensor_scalar(o1f[:, :], o1f[:, :], mcol[:, :],
                                                     rcol2[:, :], ALU.subtract, ALU.mult),
                      waits=[t_s8], inc="V")
            t_x1 = em("V", lambda e, par=par: e.tensor_tensor(x1f[:, :], o1f[:, :],
                                                              xtl[par][:, :], ALU.add),
                      waits=[t_n1, ldx[tt], s_store.get(tt - 1)], inc="V")
            s_store[tt] = em("Y", lambda e, tt=tt: e.dma_start(
                out=x1sp[tt * 128:(tt + 1) * 128, :], in_=x1f[:, :]),
                waits=[t_x1], inc="ds%d" % par, n=16)
            t_m1 = em("V", lambda e: e.tensor_reduce(mcol[:, :], x1f[:, :], AX.X, ALU.add),
                      waits=[t_x1], inc="V")
            t_m2 = em("S", lambda e: e.activation(sqc[:, :], x1f[:, :], AF.Square,
                                                  accum_out=qcol[:, :]),
                      waits=[t_x1, t_s2, t_s5], inc="S")
            t_m3 = em("V", lambda e: e.tensor_scalar(mcol[:, :], mcol[:, :], 1.0 / D, None,
                                                     ALU.mult), waits=[t_m1], inc="V")
            t_m4 = em("V", lambda e: e.tensor_tensor(vcol[:, :], mcol[:, :], mcol[:, :],
                                                     ALU.mult), waits=[t_m3], inc="V")
            t_m5 = em("V", lambda e: e.tensor_scalar(qcol[:, :], qcol[:, :], 1.0 / D, None,
                                                     ALU.mult), waits=[t_m2], inc="V")
            t_m6 = em("V", lambda e: e.tensor_tensor(vcol[:, :], qcol[:, :], vcol[:, :],
                                                     ALU.subtract), waits=[t_m4, t_m5], inc="V")
            t_m7 = em("S", lambda e: e.activation(vcol[:, :], vcol[:, :], AF.Ln,
                                                  bias=csc2[:, :]), waits=[t_m6], inc="S")
            t_m8 = em("S", lambda e: e.activation(rcol2[:, :], vcol[:, :], AF.Exp, scale=-0.5),
                      waits=[t_m7], inc="S")
            t_n2 = em("V", lambda e: e.tensor_scalar(xnb[:, :], x1f[:, :], mcol[:, :],
                                                     rcol2[:, :], ALU.subtract, ALU.mult),
                      waits=[t_m8], inc="V")
            cp_tags = {}
            for kk in range(DK):
                bank = pbanks[4 + (kk % 2)]
                em("T", lambda e, kk=kk, bank=bank: e.transpose(
                    bank.ap().bitcast(BF16)[:, 0:128],
                    xnb[:, kk * 128:(kk + 1) * 128], idc[:, :]),
                    waits=[t_n2, cp_tags.get(kk - 2)] + (prev_trans.get(tt - 1, []) if kk < 2 else []),
                    inc="T")
                tb = k.now("T")
                cp_tags[kk] = em("V", lambda e, kk=kk, bank=bank: e.tensor_copy(
                    xnt[:, kk * 128:(kk + 1) * 128], bank.ap().bitcast(BF16)[:, 0:128]),
                    waits=[tb, n_store.get(tt - 1)], inc="V")
            t_tn = k.now("V")
            n_store[tt] = em("Y", lambda e, tt=tt: e.dma_start(
                out=xn2T.rearrange("(kk p) t -> p kk t", p=128)[:, :, tt * 128:(tt + 1) * 128],
                in_=xnt[:, :].rearrange("p (kk t) -> p kk t", t=128)),
                waits=[t_tn, n_store.get(tt - 2)], inc="dn%d" % par, n=16)
            prev_sq[tt] = [t_m2, t_m5, t_n2]
            prev_trans[tt] = [cp_tags[DK - 2], cp_tags[DK - 1]]
        end_prev = [k.now(s) for s in ("T", "V", "S", "P")]
        st1.close()
        return end_prev

    end_c1 = c1()

    t_allx1 = [k.now("ds0"), k.now("ds1")]
    t_alln = [k.now("dn0"), k.now("dn1")]

    st2 = contextlib.ExitStack()
    sb2 = lambda name, shape, dt: st2.enter_context(nc.sbuf_tensor(name, shape, dt))
    HCB = 1
    xnp = [sb2("xnp%d" % i, [128, DK * TPP * 128], BF16) for i in range(2)]
    hidb = sb2("hidb", [128, HK * TPP * 128], BF16)
    w1c = [sb2("w1c%d" % i, [128, DK * 128], BF16) for i in range(2)]
    w3c = [sb2("w3c%d" % i, [128, DK * 128], BF16) for i in range(2)]
    w2c = [sb2("w2c%d" % i, [128, Dout // 2], BF16) for i in range(2)]
    o2acc = sb2("o2acc", [128, TPP * Dout], F32)
    ebf = sb2("ebf", [128, TPP * 128], BF16)
    den = sb2("den", [128, TPP * 128], F32)
    hv = sb2("hv", [128, TPP * 128], F32)
    o2f = sb2("o2f", [128, Dout], F32)
    sq2 = sb2("sq2", [128, Dout], BF16)
    x1r = [sb2("x1r%d" % i, [128, D], F32) for i in range(2)]
    ofin = sb2("ofin", [128, Dout], F32)
    mc2 = sb2("mc2", [128, 1], F32)
    qc2 = sb2("qc2", [128, 1], F32)
    vc2 = sb2("vc2", [128, 1], F32)
    rc2 = sb2("rc2", [128, 1], F32)

    k.fence(end_c1 + t_allx1 + t_alln)
    end_prev = [k.now(s) for s in ("T", "V", "S", "P")]
    o_store = {}
    for pss in range(NPASS):
        ppar = pss % 2
        tiles = list(range(pss * TPP, min((pss + 1) * TPP, NT)))
        ntl = len(tiles)
        TWID = ntl * 128
        k.fence(end_prev)
        t_xp = em("Y", lambda e, t0=tiles[0] * 128, ntl=ntl, ppar=ppar: e.dma_start(
            out=xnp[ppar][:, 0:DK * ntl * 128].rearrange("p (kk t) -> p kk t", t=ntl * 128),
            in_=xn2T.rearrange("(kk p) t -> p kk t", p=128)[:, :, t0:t0 + ntl * 128]),
            inc="dxp%d" % ppar, n=16)
        w1t, w3t, gtag, ltag = {}, {}, {}, {}

        w1src = io.get("W1bf", io["W1T"])
        w3src = io.get("W3bf", io["W3T"])

        def ld_w(hc):
            wpar = hc % 2
            w1t[hc] = em("P", lambda e, hc=hc, wpar=wpar: e.dma_start(
                out=w1c[wpar][:, :].rearrange("p (kk h) -> p kk h", h=128),
                in_=w1src.rearrange("(kk p) h -> p kk h", p=128)[:, :, hc * 128:(hc + 1) * 128]),
                waits=[gtag.get(hc - 2), io.get("t_cast")], inc="dw1%d" % wpar, n=16)
            w3t[hc] = em("P", lambda e, hc=hc, wpar=wpar: e.dma_start(
                out=w3c[wpar][:, :].rearrange("p (kk h) -> p kk h", h=128),
                in_=w3src.rearrange("(kk p) h -> p kk h", p=128)[:, :, hc * 128:(hc + 1) * 128]),
                waits=[ltag.get(hc - 2), io.get("t_cast")], inc="dw3%d" % wpar, n=16)

        ld_w(0)
        hfree = {}
        prev_d = None
        for hc in range(HK):
            wpar = hc % 2
            if hc + 1 < HK:
                ld_w(hc + 1)
            bg = pbanks[2 * (hc % 2)]
            bl = pbanks[2 * (hc % 2) + 1]
            for kk in range(DK):
                em("T", lambda e, kk=kk, hc=hc, wpar=wpar, ppar=ppar, bg=bg, TWID=TWID: e.matmul(
                    bg[:, 0:TWID], w1c[wpar][:, kk * 128:(kk + 1) * 128],
                    xnp[ppar][:, kk * TWID:(kk + 1) * TWID],
                    start=(kk == 0), stop=(kk == DK - 1)),
                    waits=[w1t[hc], t_xp, hfree.get(hc - 2)] if kk == 0 else [],
                    inc="T" if kk == DK - 1 else None)
            t_g = k.now("T")
            gtag[hc] = t_g
            for kk in range(DK):
                em("T", lambda e, kk=kk, hc=hc, wpar=wpar, ppar=ppar, bl=bl, TWID=TWID: e.matmul(
                    bl[:, 0:TWID], w3c[wpar][:, kk * 128:(kk + 1) * 128],
                    xnp[ppar][:, kk * TWID:(kk + 1) * TWID],
                    start=(kk == 0), stop=(kk == DK - 1)),
                    waits=[w3t[hc]] if kk == 0 else [],
                    inc="T" if kk == DK - 1 else None)
            t_l = k.now("T")
            ltag[hc] = t_l
            t_e = em("S", lambda e, bg=bg, TWID=TWID: e.activation(
                ebf[:, 0:TWID], bg[:, 0:TWID], AF.Tanh, scale=0.5),
                waits=[t_g, prev_d], inc="S")
            t_d = em("V", lambda e, bg=bg, TWID=TWID: e.tensor_scalar(
                den[:, 0:TWID], bg[:, 0:TWID], 0.5, None, ALU.mult),
                waits=[t_g, t_e], inc="V")
            t_h1 = em("V", lambda e, TWID=TWID: e.scalar_tensor_tensor(
                hv[:, 0:TWID], ebf[:, 0:TWID], 1.0, den[:, 0:TWID], ALU.add, ALU.mult),
                waits=[t_e, t_d], inc="V")
            t_h2 = em("V", lambda e, bl=bl, hc=hc, TWID=TWID: e.tensor_tensor(
                hidb[:, hc * TWID:(hc + 1) * TWID], hv[:, 0:TWID], bl[:, 0:TWID], ALU.mult),
                waits=[t_h1, t_l], inc="V")
            prev_d = t_h1
            hfree[hc] = t_h2
        t_hid = k.now("V")

        HWID = Dout // 2
        DCW2 = min(512, HWID)
        NCD = HWID // DCW2
        w2l = {}

        def ld_w2(key, hk, half, extra=None):
            wpar = hk % 2
            w2l[key] = em("Y", lambda e, hk=hk, half=half, wpar=wpar: e.dma_start(
                out=w2c[wpar][:, :].rearrange("p (r rr) -> p r rr", r=cfg.ncores // 2),
                in_=io["W2hG"].rearrange("(r h) rr -> h r rr", r=cfg.ncores)
                    [hk * 128:(hk + 1) * 128, half * (cfg.ncores // 2):(half + 1) * (cfg.ncores // 2)]),
                waits=[extra], inc="du%d" % wpar, n=16)

        t_lastmm = None
        ev_half = {}
        for half in range(2):
            mm_hk = {}
            ld_w2((half, 0), 0, half, t_lastmm)
            for hk in range(HK):
                if hk + 1 < HK:
                    ld_w2((half, hk + 1), hk + 1, half, mm_hk.get(hk - 1, t_lastmm))
                for ti in range(ntl):
                    for dc in range(NCD):
                        bank = pbanks[ti * NCD + dc]
                        em("T", lambda e, hk=hk, ti=ti, dc=dc, bank=bank, TWID=TWID: e.matmul(
                            bank[:, 0:DCW2],
                            hidb[:, hk * TWID + ti * 128:hk * TWID + ti * 128 + 128],
                            w2c[hk % 2][:, dc * DCW2:(dc + 1) * DCW2],
                            start=(hk == 0), stop=(hk == HK - 1)),
                            waits=([w2l[(half, hk)], t_hid] +
                                   ([ev_half[0]] if (half == 1 and hk == 0) else []))
                                  if (ti == 0 and dc == 0) else [],
                            inc="T" if (ti == ntl - 1 and dc == NCD - 1) else None)
                mm_hk[hk] = k.now("T")
            t_lastmm = mm_hk[HK - 1]
            ev2 = []
            for ti in range(ntl):
                for dc in range(NCD):
                    ev2.append(em("V", lambda e, ti=ti, dc=dc, half=half: e.tensor_copy(
                        o2acc[:, ti * Dout + half * HWID + dc * DCW2:
                              ti * Dout + half * HWID + (dc + 1) * DCW2],
                        pbanks[ti * NCD + dc][:, 0:DCW2]),
                        waits=[t_lastmm] if (ti == 0 and dc == 0) else [], inc="V"))
            ev_half[half] = ev2[-1]
        for ti in range(ntl):
            tt = tiles[ti]
            par2 = tt % 2
            ov = o2acc[:, ti * Dout:(ti + 1) * Dout]
            t_lx = em("Y", lambda e, tt=tt, par2=par2: e.dma_start(
                out=x1r[par2][:, :], in_=x1sp[tt * 128:(tt + 1) * 128, :]),
                waits=[o_store.get(tt - 2)], inc="dxr%d" % par2, n=16)
            t_a1 = em("V", lambda e, ov=ov: e.tensor_reduce(mc2[:, :], ov, AX.X, ALU.add),
                      waits=[ev_half[0], ev_half[1]], inc="V")
            t_a2 = em("S", lambda e, ov=ov: e.activation(sq2[:, :], ov, AF.Square,
                                                  accum_out=qc2[:, :]),
                      waits=[ev_half[0], ev_half[1]], inc="S")
            t_a3 = em("V", lambda e: e.tensor_scalar(mc2[:, :], mc2[:, :], 1.0 / Dout, None,
                                                     ALU.mult), waits=[t_a1], inc="V")
            t_a4 = em("V", lambda e: e.tensor_tensor(vc2[:, :], mc2[:, :], mc2[:, :],
                                                     ALU.mult), waits=[t_a3], inc="V")
            t_a5 = em("V", lambda e: e.tensor_scalar(qc2[:, :], qc2[:, :], 1.0 / Dout, None,
                                                     ALU.mult), waits=[t_a2], inc="V")
            t_a6 = em("V", lambda e: e.tensor_tensor(vc2[:, :], qc2[:, :], vc2[:, :],
                                                     ALU.subtract), waits=[t_a4, t_a5], inc="V")
            t_a7 = em("S", lambda e: e.activation(vc2[:, :], vc2[:, :], AF.Ln,
                                                  bias=csc2[:, :]), waits=[t_a6], inc="S")
            t_a8 = em("S", lambda e: e.activation(rc2[:, :], vc2[:, :], AF.Exp, scale=-0.5),
                      waits=[t_a7], inc="S")
            t_a9 = em("V", lambda e, ov=ov: e.tensor_scalar(ov, ov, mc2[:, :],
                                                     rc2[:, :], ALU.subtract, ALU.mult),
                      waits=[t_a8], inc="V")
            t_fo = em("V", lambda e, par2=par2, ov=ov: e.tensor_tensor(
                ofin[:, :], ov, x1r[par2][:, :], ALU.add),
                waits=[t_a9, t_lx, o_store.get(tt - 1)], inc="V")
            o_store[tt] = em("Y", lambda e, tt=tt, par2=par2: e.dma_start(
                out=io["outsp"][tt * 128:(tt + 1) * 128, :], in_=ofin[:, :]),
                waits=[t_fo], inc="do%d" % par2, n=16)
            k.fence([t_fo], engines=("V", "S"))
        end_prev = [k.now(s) for s in ("T", "V", "S", "P")]
    st2.close()
    return [k.now("do0"), k.now("do1")]


def build_launch1(cfg):
    nc = bass.Bass("TRN2", target_bir_lowering=False)
    k = K(nc)
    RROWS = cfg.Dout // cfg.ncores
    io = {}
    io["xT"] = nc.dram_tensor("xT", [cfg.D, cfg.Ltot], F32, kind="ExternalInput").ap()
    io["WcsT"] = nc.dram_tensor("WcsT", [cfg.D, cfg.WROWS], F32, kind="ExternalInput").ap()
    for nm, shape in [("Alog", [1, cfg.HPC]), ("Bnw", [1, cfg.N]), ("Cnw", [1, cfg.N]),
                      ("Bb", [1, cfg.HPC * cfg.N]), ("Cb", [1, cfg.HPC * cfg.N])]:
        io[nm] = nc.dram_tensor(nm, shape, F32, kind="ExternalInput").ap()
    io["W2s"] = nc.dram_tensor("W2s", [RROWS, cfg.HID], F32, kind="ExternalInput").ap()
    io["yzS"] = nc.dram_tensor("yzS", [cfg.HPC * cfg.P, cfg.Ltot], BF16,
                               kind="ExternalOutput").ap()
    io["W2hs"] = nc.dram_tensor("W2hs", [cfg.HID, RROWS], BF16, kind="ExternalOutput").ap()
    with contextlib.ExitStack() as stw:
        iow = dict(io)
        iow["stack"] = stw
        tw = emit_phaseW(k, cfg, iow)
    k.fence([k.now(s) for s in ("T", "V", "S", "P")] + tw)
    with contextlib.ExitStack() as sta:
        ioa = dict(io)
        ioa["stack"] = sta
        ta = emit_phaseA(k, cfg, ioa)
        k.emit("Y", None, waits=ta + tw)
        k.replay()
    return nc


def build_launch2(cfg):
    nc = bass.Bass("TRN2", target_bir_lowering=False)
    k = K(nc)
    RROWS = cfg.Dout // cfg.ncores
    io = {}
    io["yzsp"] = nc.dram_tensor("yzsp", [cfg.DIN, cfg.SPAN], BF16, kind="ExternalInput").ap()
    io["WoutT"] = nc.dram_tensor("WoutT", [cfg.DIN, cfg.Dout], F32, kind="ExternalInput").ap()
    io["W1T"] = nc.dram_tensor("W1T", [cfg.D, cfg.HID], F32, kind="ExternalInput").ap()
    io["W3T"] = nc.dram_tensor("W3T", [cfg.D, cfg.HID], F32, kind="ExternalInput").ap()
    io["W2hG"] = nc.dram_tensor("W2hG", [cfg.ncores * cfg.HID, RROWS], BF16,
                                kind="ExternalInput").ap()
    io["xsp"] = nc.dram_tensor("xsp", [cfg.SPAN, cfg.D], F32, kind="ExternalInput").ap()
    io["outsp"] = nc.dram_tensor("outsp", [cfg.SPAN, cfg.Dout], F32, kind="ExternalOutput").ap()
    with contextlib.ExitStack() as stack:
        io["stack"] = stack
        tc = emit_phaseC(k, cfg, io)
        k.emit("Y", None, waits=tc)
        k.replay()
    return nc



def _deinterleave_rows(w, N):
    H = w.shape[0] // N
    w = w.reshape(H, N, -1)
    perm = np.concatenate([np.arange(0, N, 2), np.arange(1, N, 2)])
    return w[:, perm].reshape(H * N, -1)


def pack_core_inputs(cfg, core, ins):
    hp = cfg.HPC
    h0 = core * hp
    P, N = cfg.P, cfg.N
    f = lambda nm: np.asarray(ins[nm], np.float32)
    Wx = f("L0_attn_Wx")[h0 * P:(h0 + hp) * P]
    Wz = f("L0_attn_Wz")[h0 * P:(h0 + hp) * P]
    WB = _deinterleave_rows(f("L0_attn_WB")[h0 * N:(h0 + hp) * N], N)
    WC = _deinterleave_rows(f("L0_attn_WC")[h0 * N:(h0 + hp) * N], N)
    Wth = f("L0_attn_Wth")[h0 * (N // 2):(h0 + hp) * (N // 2)]
    Wdt = f("L0_attn_Wdt")[h0:h0 + hp]
    Wlam = f("L0_attn_Wlam")[h0:h0 + hp]
    Wcat = np.concatenate([Wx, Wz, WB, WC, Wth, Wdt, Wlam], axis=0)
    assert Wcat.shape == (cfg.WROWS, cfg.D)
    perm = np.concatenate([np.arange(0, N, 2), np.arange(1, N, 2)])
    return {
        "WcsT": np.ascontiguousarray(Wcat.T),
        "Alog": np.ascontiguousarray(f("L0_attn_A_log")[h0:h0 + hp].reshape(1, hp)),
        "Bnw": np.ascontiguousarray(f("L0_attn_Bn_w")[perm].reshape(1, N)),
        "Cnw": np.ascontiguousarray(f("L0_attn_Cn_w")[perm].reshape(1, N)),
        "Bb": np.ascontiguousarray(f("L0_attn_B_bias")[h0:h0 + hp][:, perm].reshape(1, hp * N)),
        "Cb": np.ascontiguousarray(f("L0_attn_C_bias")[h0:h0 + hp][:, perm].reshape(1, hp * N)),
    }


def host_pack_A(cfg, ins):
    x = np.asarray(ins["L0_attn_x"], np.float32).reshape(cfg.Ltot, cfg.D)
    xT = np.ascontiguousarray(x.T)
    RROWS = cfg.Dout // cfg.ncores
    W2 = np.asarray(ins["L0_ffn_W2"], np.float32)
    maps = []
    for c in range(cfg.ncores):
        d = pack_core_inputs(cfg, c, ins)
        m = {
            "xT": xT,
            "WcsT": d["WcsT"],
            "Alog": d["Alog"], "Bnw": d["Bnw"], "Cnw": d["Cnw"],
            "Bb": d["Bb"], "Cb": d["Cb"],
            "W2s": np.ascontiguousarray(W2[c * RROWS:(c + 1) * RROWS]),
        }
        maps.append(m)
    return maps, x


def host_pack_C(cfg, ins, x, yzS_list, W2hs_list):
    WoutT = np.ascontiguousarray(np.asarray(ins["L0_attn_Wout"], np.float32).T)
    W1T = np.ascontiguousarray(np.asarray(ins["L0_ffn_W1"], np.float32).T)
    W3T = np.ascontiguousarray(np.asarray(ins["L0_ffn_W3"], np.float32).T)
    W2hG = np.ascontiguousarray(np.concatenate(W2hs_list, axis=0))
    XW = cfg.HPC * cfg.P
    maps = []
    for c in range(cfg.ncores):
        yzsp = np.ascontiguousarray(
            np.concatenate([yzS_list[r][c * XW:(c + 1) * XW] for r in range(cfg.ncores)],
                           axis=0))
        maps.append({
            "yzsp": yzsp,
            "WoutT": WoutT, "W1T": W1T, "W3T": W3T, "W2hG": W2hG,
            "xsp": np.ascontiguousarray(x[c * cfg.SPAN:(c + 1) * cfg.SPAN]),
        })
    return maps


def kernel_unfused(inputs):
    cfg = Cfg()
    maps1, x = host_pack_A(cfg, inputs)
    nc1 = build_launch1(cfg)
    res1 = run_bass_kernel_spmd(nc1, maps1, core_ids=list(range(cfg.ncores)))
    yzT_list = [r["yzS"] for r in res1.results]
    W2hs_list = [r["W2hs"] for r in res1.results]
    maps2 = host_pack_C_unfused(cfg, inputs, x, yzT_list, W2hs_list)
    nc2 = build_launch2(cfg)
    res2 = run_bass_kernel_spmd(nc2, maps2, core_ids=list(range(cfg.ncores)))
    out = np.concatenate([np.asarray(r["outsp"], np.float32) for r in res2.results], axis=0)
    return out.reshape(cfg.B, cfg.L, cfg.Dout)


def kernel(**inputs):
    return kernel_fused(inputs)


def host_pack_C_unfused(cfg, ins, x, yzT_list, W2hs_list):
    WoutT = np.ascontiguousarray(np.asarray(ins["L0_attn_Wout"], np.float32).T)
    W1T = np.ascontiguousarray(np.asarray(ins["L0_ffn_W1"], np.float32).T)
    W3T = np.ascontiguousarray(np.asarray(ins["L0_ffn_W3"], np.float32).T)
    W2hG = np.ascontiguousarray(np.concatenate(W2hs_list, axis=0))
    maps = []
    for c in range(cfg.ncores):
        yzsp = np.ascontiguousarray(
            np.concatenate([yzT_list[r][:, c * cfg.SPAN:(c + 1) * cfg.SPAN]
                            for r in range(cfg.ncores)], axis=0))
        maps.append({
            "yzsp": yzsp,
            "WoutT": WoutT, "W1T": W1T, "W3T": W3T, "W2hG": W2hG,
            "xsp": np.ascontiguousarray(x[c * cfg.SPAN:(c + 1) * cfg.SPAN]),
        })
    return maps


def build_fused(cfg):
    nc = bass.Bass("TRN2", target_bir_lowering=False, num_devices=cfg.ncores)
    k = K(nc)
    RROWS = cfg.Dout // cfg.ncores
    XW = cfg.HPC * cfg.P
    io = {}
    io["xT"] = nc.dram_tensor("xT", [cfg.D, cfg.Ltot], F32, kind="ExternalInput").ap()
    io["WcsT"] = nc.dram_tensor("WcsT", [cfg.D, cfg.WROWS], F32, kind="ExternalInput").ap()
    for nm, shape in [("Alog", [1, cfg.HPC]), ("Bnw", [1, cfg.N]), ("Cnw", [1, cfg.N]),
                      ("Bb", [1, cfg.HPC * cfg.N]), ("Cb", [1, cfg.HPC * cfg.N])]:
        io[nm] = nc.dram_tensor(nm, shape, F32, kind="ExternalInput").ap()
    io["W2s"] = nc.dram_tensor("W2s", [RROWS, cfg.HID], F32, kind="ExternalInput").ap()
    io["WoutT"] = nc.dram_tensor("WoutT", [cfg.DIN, cfg.Dout], F32, kind="ExternalInput").ap()
    io["W1T"] = nc.dram_tensor("W1T", [cfg.D, cfg.HID], F32, kind="ExternalInput").ap()
    io["W3T"] = nc.dram_tensor("W3T", [cfg.D, cfg.HID], F32, kind="ExternalInput").ap()
    io["xsp"] = nc.dram_tensor("xsp", [cfg.SPAN, cfg.D], F32, kind="ExternalInput").ap()
    io["outsp"] = nc.dram_tensor("outsp", [cfg.SPAN, cfg.Dout], F32,
                                 kind="ExternalOutput").ap()
    yzT_loc = nc.dram_tensor("yzT_loc", [XW, cfg.Ltot], BF16, kind="Internal").ap()
    yzA = nc.dram_tensor("yzA", [cfg.ncores * XW, cfg.Ltot], BF16, kind="Internal",
                         addr_space="Shared").ap()
    w2hs_loc = nc.dram_tensor("w2hs_loc", [cfg.HID, RROWS], BF16, kind="Internal").ap()
    w2hG = nc.dram_tensor("w2hG", [cfg.ncores * cfg.HID, RROWS], BF16, kind="Internal",
                          addr_space="Shared").ap()
    io["yzS"] = yzT_loc
    io["W2hs"] = w2hs_loc
    io["yzsp"] = yzA
    io["W2hG"] = w2hG

    with contextlib.ExitStack() as stw:
        iow = dict(io)
        iow["stack"] = stw
        tw = emit_phaseW(k, cfg, iow)
    rg = [list(range(cfg.ncores))]
    k.emit("P", lambda e: e.collective_compute(
        "AllGather", ALU.bypass, replica_groups=rg,
        ins=[w2hs_loc], outs=[w2hG]), waits=tw, inc="cc", n=1)
    t_cc1 = k.now("cc")
    w1bf = nc.dram_tensor("w1bf", [cfg.D, cfg.HID], BF16, kind="Internal").ap()
    w3bf = nc.dram_tensor("w3bf", [cfg.D, cfg.HID], BF16, kind="Internal").ap()
    wobf = nc.dram_tensor("wobf", [cfg.DIN, cfg.Dout], BF16, kind="Internal").ap()
    k.new_sem("dct")
    for r0 in range(0, cfg.D, 128):
        k.emit("P", lambda e, r0=r0: e.dma_start(
            out=w1bf[r0:r0 + 128, :], in_=io["W1T"][r0:r0 + 128, :]), inc="dct", n=16)
        k.emit("P", lambda e, r0=r0: e.dma_start(
            out=w3bf[r0:r0 + 128, :], in_=io["W3T"][r0:r0 + 128, :]), inc="dct", n=16)
    for r0 in range(0, cfg.DIN, 128):
        k.emit("P", lambda e, r0=r0: e.dma_start(
            out=wobf[r0:r0 + 128, :], in_=io["WoutT"][r0:r0 + 128, :]), inc="dct", n=16)
    io["t_cast"] = k.now("dct")
    io["W1bf"], io["W3bf"], io["Woutbf"] = w1bf, w3bf, wobf
    k.fence([k.now(s) for s in ("T", "V", "S", "P")] + tw)
    with contextlib.ExitStack() as sta:
        ioa = dict(io)
        ioa["stack"] = sta
        ta = emit_phaseA(k, cfg, ioa)
    k.emit("P", lambda e: e.collective_compute(
        "AllGather", ALU.bypass, replica_groups=rg,
        ins=[yzT_loc], outs=[yzA]), waits=ta, inc="cc", n=1)
    t_cc2 = k.now("cc")
    k.fence([k.now(s) for s in ("T", "V", "S", "P")] + [t_cc1])
    io["t_yzA"] = t_cc2
    io["ta_prev"] = ta
    io["pid"] = nc.partition_id()
    with contextlib.ExitStack() as stc:
        ioc = dict(io)
        ioc["stack"] = stc
        tc = emit_phaseC(k, cfg, ioc)
        k.emit("Y", None, waits=tc)
        k.replay()
    return nc


def kernel_fused(inputs):
    cfg = Cfg()
    maps1, x = host_pack_A(cfg, inputs)
    WoutT = np.ascontiguousarray(np.asarray(inputs["L0_attn_Wout"], np.float32).T)
    W1T = np.ascontiguousarray(np.asarray(inputs["L0_ffn_W1"], np.float32).T)
    W3T = np.ascontiguousarray(np.asarray(inputs["L0_ffn_W3"], np.float32).T)
    for c in range(cfg.ncores):
        maps1[c].update({
            "WoutT": WoutT, "W1T": W1T, "W3T": W3T,
            "xsp": np.ascontiguousarray(x[c * cfg.SPAN:(c + 1) * cfg.SPAN]),
        })
    nc = build_fused(cfg)
    res = run_bass_kernel_spmd(nc, maps1, core_ids=list(range(cfg.ncores)))
    out = np.concatenate([np.asarray(r["outsp"], np.float32) for r in res.results], axis=0)
    return out.reshape(cfg.B, cfg.L, cfg.Dout)
